# version 13 (final)
# baseline (speedup 1.0000x reference)
import math
import numpy as np
import ml_dtypes

import os
import concourse.bass as bass
import concourse.mybir as mybir
from concourse.bass_utils import run_bass_kernel_spmd
from concourse.bass_core_rust import IndirectOffsetOnAxis

F32 = mybir.dt.float32
BF16 = mybir.dt.bfloat16
I32 = mybir.dt.int32
AF = mybir.ActivationFunctionType
ALU = mybir.AluOpType
AX = mybir.AxisListType

NCORE = 8
S = 8192
D = 4096
HEADS = 32
HD = 128
ROWS = 1024
NTILE = 8
SCALE = float(HD) ** -0.5
EPS = 1e-5
T_CUT = 80.0
SEED_MAX = 10
NEG = -1e30
BF = ml_dtypes.bfloat16


def _slopes():
    base = 2.0 ** (-(2.0 ** -(math.log2(HEADS) - 3)))
    return (base ** np.arange(1, HEADS + 1, dtype=np.float64)).astype(np.float32)


SLOPES = _slopes()
L_H = [min(4 + int(math.ceil((T_CUT / float(SLOPES[h])) / 128.0)), 64)
       for h in range(HEADS)]


ENGS = ("tensor", "scalar", "vector", "sync", "gpsimd")


class Plan:
    def __init__(self, nc):
        self.nc = nc
        self.stream = {e: [] for e in ENGS}
        self.sems = {}
        self.cnt = {}
        self._ctx = []

    def sem(self, name):
        cm = self.nc.semaphore(name)
        h = cm.__enter__()
        self._ctx.append(cm)
        self.sems[name] = h
        self.cnt[name] = 0

    def op(self, eng, fn, inc=None, incv=1):
        if inc is None:
            self.stream[eng].append(lambda e, fn=fn: fn(e))
            return None
        h = self.sems[inc]
        if incv == "dflt":
            self.cnt[inc] += 1
            self.stream[eng].append(lambda e, fn=fn, h=h: fn(e).then_inc(h))
        else:
            self.cnt[inc] += incv
            self.stream[eng].append(
                lambda e, fn=fn, h=h, v=incv: fn(e).then_inc(h, v))
        return self.cnt[inc]

    def wait(self, eng, name, val=None):
        v = self.cnt[name] if val is None else val
        if v <= 0:
            return
        h = self.sems[name]
        self.stream[eng].append(lambda e, h=h, v=v: e.wait_ge(h, v))

    def barrier(self, name="ph"):
        for e in ENGS:
            self.op(e, lambda en: en.nop(), inc=name)
        tgt = self.cnt[name]
        for e in ENGS:
            self.wait(e, name, tgt)
            for d in ("d_park", "d_out", "d_kv"):
                self.wait(e, d)


def build_program():
    nc = bass.Bass("TRN2", target_bir_lowering=False)
    p = Plan(nc)
    ctx = p._ctx

    def sbuf(name, shape, dt):
        cm = nc.sbuf_tensor(name, shape, dt)
        h = cm.__enter__()
        return h, cm

    def psum(name, shape):
        cm = nc.psum_tensor(name, shape, F32)
        h = cm.__enter__()
        ctx.append(cm)
        return h

    x_in = nc.dram_tensor("x_sh", [ROWS, D], F32, kind="ExternalInput").ap()
    wqkv_in = nc.dram_tensor("wqkv_t", [34, 32, 128, 128], F32, kind="ExternalInput").ap()
    wo_in = nc.dram_tensor("wo_t", [8, 32, 128, 512], F32, kind="ExternalInput").ap()
    wffn_in = nc.dram_tensor("wffn_t", [32, 128, D], F32, kind="ExternalInput").ap()
    gamma_in = nc.dram_tensor("gamma_b", [128, D], F32, kind="ExternalInput").ap()
    beta_in = nc.dram_tensor("beta_b", [128, D], F32, kind="ExternalInput").ap()
    bias_in = nc.dram_tensor("bias_tab", [128, 2 * HEADS * 64], F32, kind="ExternalInput").ap()
    kvidx_in = nc.dram_tensor("kv_idx", [128, 2 * 64], I32, kind="ExternalInput").ap()
    hbd_in = nc.dram_tensor("hbd", [128, 128], BF16, kind="ExternalInput").ap()
    ident_in = nc.dram_tensor("ident", [128, 128], BF16, kind="ExternalInput").ap()
    trineg_in = nc.dram_tensor("trineg", [4, 128, 512], BF16, kind="ExternalInput").ap()
    dbase_in = nc.dram_tensor("dbase", [128, 512], F32, kind="ExternalInput").ap()
    onescb_in = nc.dram_tensor("ones_col_bf", [128, 1], BF16, kind="ExternalInput").ap()
    onesc32_in = nc.dram_tensor("ones_col_f32", [128, 1], F32, kind="ExternalInput").ap()
    onesr32_in = nc.dram_tensor("ones_row_f32", [1, 128], F32, kind="ExternalInput").ap()
    out_d = nc.dram_tensor("out_sh", [ROWS, D], F32, kind="ExternalOutput").ap()

    DBG = os.environ.get("KDEBUG", "") == "1"
    dk = dict(kind="ExternalOutput") if DBG else {}
    park_xh = nc.dram_tensor("park_xh", [128, 32 * 1024], BF16, **dk).ap()
    park_s = nc.dram_tensor("park_s", [8, ROWS, 512], BF16, **dk).ap()
    if DBG:
        dbg_xnt = nc.dram_tensor("dbg_xnt", [128, 32 * 1024], BF16, kind="ExternalOutput").ap()
        dbg_kv = nc.dram_tensor("dbg_kv", [S, 256], BF16, kind="ExternalOutput").ap()
        dbg_att = nc.dram_tensor("dbg_att", [128, 32 * 1024], BF16, kind="ExternalOutput").ap()
    kv_cc_in = nc.dram_tensor("kv_cc_in", [ROWS, 256], BF16)
    kv_cc_out = nc.dram_tensor("kv_cc_out", [S, 256], BF16)

    for s_ in ("d_in", "d_w", "d_wg0", "d_wg1", "d_kv", "d_park", "d_out", "cc",
               "mm", "sc", "ve", "gp", "gpk0", "gpk1", "d_bias", "d_ws0", "d_ws1",
               "d_f0", "d_f1", "d_fx", "d_kvi", "gpe", "ph"):
        p.sem(s_)

    bigA, _ = sbuf("bigA", [128, 32 * 1024], BF16)
    bigB, _ = sbuf("bigB", [128, 32 * 1024], BF16)
    hbd, _ = sbuf("hbd_s", [128, 128], BF16)
    ident, _ = sbuf("ident_s", [128, 128], BF16)
    onescb, _ = sbuf("onescb_s", [128, 1], BF16)
    onesc32, _ = sbuf("onesc32_s", [128, 1], F32)
    onesr32, _ = sbuf("onesr32_s", [1, 128], F32)
    stats, _ = sbuf("stats", [128, 64], F32)
    smalls, _ = sbuf("smalls", [128, 24], F32)
    wq0, cm_wq0 = sbuf("wq0", [128, 2 * 32 * 128], BF16)
    PS = [psum(f"ps{i}", [128, 512]) for i in range(8)]

    p.op("gpsimd", lambda e: e.dma_start(
        out=wq0[:, :].rearrange("p (c d f) -> p c d f", c=2, d=32),
        in_=wqkv_in[32:34].rearrange("c d p f -> p c d f")),
        inc="d_wg0", incv=16)

    xtile, cm_xtile = sbuf("xtile", [128, D], F32)
    work, cm_work = sbuf("work", [128, D], F32)
    work2, cm_work2 = sbuf("work2", [128, D], F32)
    wkb, cm_wkb = sbuf("wkb", [128, D], BF16)
    wkb2 = bigB[:, 16 * 1024:20 * 1024]
    tmpa = work[:, :].bitcast(BF16)[:, 0:D]
    tmpb = work2[:, :].bitcast(BF16)[:, 0:D]
    gb_sb = bigB[:, :].bitcast(F32)[:, 0:D]
    bb_sb = bigB[:, :].bitcast(F32)[:, D:2 * D]

    for dst, src in [(hbd, hbd_in), (ident, ident_in),
                     (onescb, onescb_in), (onesc32, onesc32_in),
                     (onesr32, onesr32_in), (gb_sb, gamma_in), (bb_sb, beta_in)]:
        p.op("sync", lambda e, d=dst, s=src: e.dma_start(out=d[:, :], in_=s),
             inc="d_in", incv=16)
    p.op("sync", lambda e: e.dma_start(out=xtile[:, :], in_=x_in[0:128, :]),
         inc="d_in", incv=16)
    n_const = p.cnt["d_in"]
    p.wait("tensor", "d_in", n_const)
    p.wait("scalar", "d_in", n_const)

    def vch(fn):
        n = p.op("vector", fn, inc="ve")
        p.wait("vector", "ve", n)
        return n

    for t in range(NTILE):
        p.wait("vector", "d_in", n_const + 16 * t)
        if t >= 1:
            p.wait("vector", "d_park")
        for k in range(8):
            if k < 7:
                p.op("vector", lambda e, k=k: e.bn_stats(
                    stats[:, 6 * k:6 * k + 6], xtile[:, 512 * k:512 * (k + 1)]))
            else:
                vch(lambda e, k=k: e.bn_stats(
                    stats[:, 6 * k:6 * k + 6], xtile[:, 512 * k:512 * (k + 1)]))
        vch(lambda e: e.bn_aggr(smalls[:, 0:2], stats[:, 0:48]))
        vch(lambda e: e.tensor_scalar(smalls[:, 2:3], smalls[:, 1:2], EPS, None, ALU.add))
        p.wait("scalar", "ve")
        n = p.op("scalar", lambda e: e.activation(smalls[:, 3:4], smalls[:, 2:3], AF.Sqrt),
                 inc="sc")
        p.wait("vector", "sc", n)
        vch(lambda e: e.reciprocal(smalls[:, 4:5], smalls[:, 3:4]))
        vch(lambda e: e.tensor_mul(smalls[:, 5:6], smalls[:, 0:1], smalls[:, 4:5]))
        vch(lambda e: e.tensor_scalar(smalls[:, 6:7], smalls[:, 5:6], -1.0, None, ALU.mult))
        p.wait("scalar", "ve")
        n = p.op("scalar", lambda e: e.activation(
            work[:, :], xtile[:, :], AF.Identity, bias=smalls[:, 6:7], scale=smalls[:, 4:5]),
            inc="sc")
        p.wait("vector", "sc", n)
        nwkb = vch(lambda e: e.tensor_copy(wkb[:, :], work[:, :]))
        if t + 1 < NTILE:
            p.wait("sync", "sc", n)
            p.op("sync", lambda e, t=t: e.dma_start(
                out=xtile[:, :], in_=x_in[128 * (t + 1):128 * (t + 2), :]),
                inc="d_in", incv=16)
        for k in range(8):
            if k < 7:
                p.op("vector", lambda e, k=k: e.bn_stats(
                    stats[:, 6 * k:6 * k + 6], work[:, 512 * k:512 * (k + 1)]))
            else:
                vch(lambda e, k=k: e.bn_stats(
                    stats[:, 6 * k:6 * k + 6], work[:, 512 * k:512 * (k + 1)]))
        vch(lambda e: e.bn_aggr(smalls[:, 8:10], stats[:, 0:48]))
        vch(lambda e: e.tensor_scalar(smalls[:, 10:11], smalls[:, 9:10], EPS, None, ALU.add))
        p.wait("scalar", "ve")
        n = p.op("scalar", lambda e: e.activation(smalls[:, 11:12], smalls[:, 10:11], AF.Sqrt),
                 inc="sc")
        p.wait("vector", "sc", n)
        vch(lambda e: e.reciprocal(smalls[:, 12:13], smalls[:, 11:12]))
        vch(lambda e: e.tensor_mul(smalls[:, 13:14], smalls[:, 8:9], smalls[:, 12:13]))
        vch(lambda e: e.tensor_scalar(smalls[:, 14:15], smalls[:, 13:14], -1.0, None, ALU.mult))
        p.wait("scalar", "ve")
        n = p.op("scalar", lambda e: e.activation(
            work2[:, :], work[:, :], AF.Identity, bias=smalls[:, 14:15], scale=smalls[:, 12:13]),
            inc="sc")
        p.wait("vector", "sc", n)
        vch(lambda e: e.tensor_mul(work[:, :], work2[:, :], gb_sb))
        vch(lambda e: e.tensor_add(work2[:, :], work[:, :], bb_sb))
        vch(lambda e: e.tensor_reduce(smalls[:, 7:8], work2[:, :], AX.X, ALU.max,
                                      apply_absolute_value=True))
        vch(lambda e: e.tensor_scalar(smalls[:, 7:8], smalls[:, 7:8], 1e-6, 1e6,
                                      ALU.max, ALU.min))
        vch(lambda e: e.reciprocal(smalls[:, 7:8], smalls[:, 7:8]))
        vch(lambda e: e.tensor_scalar(smalls[:, 7:8], smalls[:, 7:8], 7.0, None, ALU.mult))
        vch(lambda e: e.tensor_scalar(smalls[:, 7:8], smalls[:, 7:8], 1e-6, 1e6,
                                      ALU.max, ALU.min))
        p.wait("scalar", "ve")
        n = p.op("scalar", lambda e: e.activation(
            work[:, :], work2[:, :], AF.Identity, scale=smalls[:, 7:8]), inc="sc")
        p.wait("vector", "sc", n)
        vch(lambda e: e.tensor_copy(work2[:, :].bitcast(I32), work[:, :]))
        vch(lambda e: e.tensor_copy(wkb2[:, :], work2[:, :].bitcast(I32)))

        def stage(src_get, dst_put, kind, banks, dst_batched=None):
            evn = [None] * 8
            for grp in range(8):
                bank = PS[banks[grp % 2]]
                if grp >= 2:
                    p.wait("tensor", evn[grp - 2][0], evn[grp - 2][1])
                for q4 in range(4):
                    ci = grp * 4 + q4
                    if kind == "tr":
                        p.op("tensor", lambda e, b=bank, q4=q4, ci=ci: e.matmul(
                            b[:, 128 * q4:128 * (q4 + 1)], src_get(ci), ident[:, :],
                            start=True, stop=True),
                            inc="mm" if q4 == 3 else None)
                    else:
                        p.op("tensor", lambda e, b=bank, q4=q4, ci=ci: e.matmul(
                            b[:, 128 * q4:128 * (q4 + 1)], hbd[:, :], src_get(ci),
                            start=True, stop=True),
                            inc="mm" if q4 == 3 else None)
                if dst_batched is not None:
                    eng = "vector" if grp % 2 == 0 else "scalar"
                    sem = "ve" if eng == "vector" else "sc"
                    p.wait(eng, "mm")
                    if eng == "vector":
                        n = p.op(eng, lambda e, b=bank, grp=grp: e.tensor_copy(
                            dst_batched(grp), b[:, :]), inc=sem)
                    else:
                        n = p.op(eng, lambda e, b=bank, grp=grp: e.activation(
                            dst_batched(grp), b[:, :], AF.Copy), inc=sem)
                    evn[grp] = (sem, n)
                else:
                    p.wait("vector", "mm")
                    nv = None
                    for q4 in range(4):
                        ci = grp * 4 + q4
                        nv = p.op("vector", lambda e, b=bank, q4=q4, ci=ci: e.tensor_copy(
                            dst_put(ci), b[:, 128 * q4:128 * (q4 + 1)]),
                            inc="ve" if q4 == 3 else None)
                    evn[grp] = ("ve", nv)
            p.wait("tensor", evn[6][0], evn[6][1])
            p.wait("tensor", evn[7][0], evn[7][1])

        p.wait("tensor", "ve", nwkb)
        bigA_v = bigA[:, :].rearrange("p (c q) -> p c q", c=32)
        stage(lambda ci: wkb[:, 128 * ci:128 * (ci + 1)], None, "tr", (0, 1),
              dst_batched=lambda grp, t=t: bigA_v[:, 4 * grp:4 * (grp + 1),
                                                  128 * t:128 * (t + 1)])
        stage(lambda ci: wkb2[:, 128 * ci:128 * (ci + 1)], None, "tr", (2, 3),
              dst_batched=lambda grp: tmpa[:, 512 * grp:512 * (grp + 1)])
        stage(lambda ci: tmpa[:, 128 * ci:128 * (ci + 1)], None, "hbd", (4, 5),
              dst_batched=lambda grp: tmpb[:, 512 * grp:512 * (grp + 1)])
        wkb_v = wkb.ap().rearrange("p (j m i2) -> p m j i2", j=64, m=32, i2=2)
        evn = [None] * 8
        for grp in range(8):
            bank = PS[6 + grp % 2]
            if grp >= 2:
                p.wait("tensor", "ve", evn[grp - 2])
            for q4 in range(4):
                ci = grp * 4 + q4
                p.op("tensor", lambda e, b=bank, q4=q4, ci=ci: e.matmul(
                    b[:, 128 * q4:128 * (q4 + 1)],
                    tmpb[:, 128 * ci:128 * (ci + 1)], ident[:, :],
                    start=True, stop=True), inc="mm" if q4 == 3 else None)
            p.wait("vector", "mm")
            nv = None
            for q4 in range(4):
                ci = grp * 4 + q4
                src = PS[6 + grp % 2].ap().rearrange(
                    "p (m i2 j) -> p m j i2", m=4, i2=2, j=64)[:, q4:q4 + 1]
                dst = wkb_v[:, ci:ci + 1]
                nv = p.op("vector", lambda e, s=src, d=dst: e.tensor_copy(d, s),
                          inc="ve" if q4 == 3 else None)
            evn[grp] = nv
        p.wait("tensor", "ve", evn[7])
        stage(lambda ci: wkb[:, 128 * ci:128 * (ci + 1)], None, "tr", (0, 1),
              dst_batched=lambda grp: tmpa[:, 512 * grp:512 * (grp + 1)])
        stage(lambda ci: tmpa[:, 128 * ci:128 * (ci + 1)], None, "hbd", (2, 3),
              dst_batched=lambda grp: wkb2[:, 512 * grp:512 * (grp + 1)])
        p.wait("sync", "ve")
        p.wait("sync", "sc")
        for q8 in range(4):
            p.op("sync", lambda e, t=t, q8=q8: e.dma_start(
                out=park_xh[:, :].rearrange("p (g c q) -> p g c q", g=1, c=32)[
                    :, :, 8 * q8:8 * (q8 + 1), 128 * t:128 * (t + 1)],
                in_=bigB[:, :].rearrange("p (g c q) -> p g c q", g=8, c=32)[
                    :, 4:5, 8 * q8:8 * (q8 + 1), :]),
                inc="d_park", incv=16)

    if DBG:
        p.wait("sync", "ve")
        p.op("sync", lambda e: e.dma_start(out=dbg_xnt, in_=bigA[:, :]),
             inc="d_out", incv=16)
    p.barrier()
    for cm in (cm_wkb, cm_work2):
        cm.__exit__(None, None, None)

    wq1, cm_wq1 = sbuf("wq1", [128, 2 * 32 * 128], BF16)
    kvt, cm_kvt = sbuf("kvt", [128, 2 * 1024], BF16)
    ccsb, cm_ccsb = sbuf("ccsb", [128, 8 * 256], BF16)
    WQ = (wq0, wq1)

    WSC = (xtile, work)
    p.op("vector", lambda e: e.memset(smalls[:, 16:17], 0.0), inc="ve")
    p.op("sync", lambda e: e.dma_start(out=xtile[:, :], in_=wffn_in[0]),
         inc="d_ws0", incv=16)
    scred = [0, 0]
    for ci in range(32):
        par = ci % 2
        if ci + 1 < 32:
            if ci >= 1:
                p.wait("sync", "ve", scred[(ci + 1) % 2])
            p.op("sync", lambda e, ci=ci, b=WSC[(ci + 1) % 2]: e.dma_start(
                out=b[:, :], in_=wffn_in[ci + 1]),
                inc="d_ws0" if (ci + 1) % 2 == 0 else "d_ws1", incv=16)
        p.wait("vector", "d_ws0" if par == 0 else "d_ws1")
        p.op("vector", lambda e, b=WSC[par]: e.tensor_reduce(
            smalls[:, 17:18], b[:, :], AX.X, ALU.add, apply_absolute_value=True),
            inc="ve")
        p.wait("vector", "ve")
        scred[par] = p.op("vector", lambda e: e.tensor_add(
            smalls[:, 16:17], smalls[:, 16:17], smalls[:, 17:18]), inc="ve")
        p.wait("vector", "ve", scred[par])

    def qkv_scale(col):
        if col < HEADS and col < SEED_MAX:
            return SCALE / float(SLOPES[col])
        return 1.0

    ev_hist = []
    p.wait("scalar", "d_park")
    def emit_kv_assembly():
        for feat in range(2):
            for tt in range(8):
                bank = PS[4 + (feat * 8 + tt) % 2]
                nmm = p.op("tensor", lambda e, b=bank, feat=feat, tt=tt: e.matmul(
                    b[:, 0:128], kvt[:, feat * 1024 + 128 * tt: feat * 1024 + 128 * (tt + 1)],
                    ident[:, :], start=True, stop=True), inc="mm")
                p.wait("vector", "mm", nmm)
                nv = p.op("vector", lambda e, b=bank, feat=feat, tt=tt: e.tensor_copy(
                    ccsb[:, 256 * tt + 128 * feat: 256 * tt + 128 * (feat + 1)],
                    b[:, 0:128]), inc="ve")
                p.wait("tensor", "ve", nv)
        p.wait("sync", "ve")
        p.op("sync", lambda e: e.dma_start(
            out=kv_cc_in.ap().rearrange("(t p) f -> p t f", p=128),
            in_=ccsb[:, :].rearrange("p (t f) -> p t f", t=8)), inc="d_kv", incv=16)
        p.wait("gpsimd", "d_kv")
        p.op("gpsimd", lambda e: e.collective_compute(
            "AllGather", ALU.bypass, replica_groups=[list(range(NCORE))],
            ins=[kv_cc_in.ap().opt()], outs=[kv_cc_out.ap().opt()]), inc="cc",
            incv="dflt")

    CGORD = [16] + list(range(16))
    for cgi in range(17):
        cg = CGORD[cgi]
        wbuf = WQ[cgi % 2]
        p.wait("tensor", "d_wg0" if cgi % 2 == 0 else "d_wg1")
        if cgi + 1 < 17:
            nbuf = WQ[(cgi + 1) % 2]
            ncg = CGORD[cgi + 1]
            if cgi == 0:
                p.wait("gpsimd", "d_park")
            if cgi >= 1:
                p.wait("gpsimd", "mm", ev_hist[cgi - 1])
            p.op("gpsimd", lambda e, ncg=ncg, nb=nbuf: e.dma_start(
                out=nb[:, :].rearrange("p (c d f) -> p c d f", c=2, d=32),
                in_=wqkv_in[2 * ncg:2 * ncg + 2].rearrange("c d p f -> p c d f")),
                inc="d_wg0" if (cgi + 1) % 2 == 0 else "d_wg1", incv=16)
        for cc in range(2):
            col = 2 * cg + cc
            for qh in range(2):
                bank = PS[2 * cc + qh]
                p.wait("tensor", "sc")
                nmm = None
                for d in range(32):
                    nmm = p.op("tensor", lambda e, b=bank, cc=cc, d=d, qh=qh,
                               w=wbuf: e.matmul(
                        b[:, :], w[:, (cc * 32 + d) * 128:(cc * 32 + d + 1) * 128],
                        bigA[:, d * 1024 + qh * 512: d * 1024 + (qh + 1) * 512],
                        start=(d == 0), stop=(d == 31)),
                        inc="mm" if d == 31 else None)
                p.wait("scalar", "mm", nmm)
                if col < HEADS:
                    p.op("scalar", lambda e, b=bank, col=col, qh=qh: e.activation(
                        bigB[:, col * 1024 + qh * 512: col * 1024 + (qh + 1) * 512],
                        b[:, :], AF.Copy), inc="sc")
                else:
                    p.op("scalar", lambda e, b=bank, col=col, qh=qh: e.activation(
                        kvt[:, (col - 32) * 1024 + qh * 512:(col - 32) * 1024 + (qh + 1) * 512],
                        b[:, :], AF.Copy), inc="sc")
        ev_hist.append(p.cnt["mm"])
        if cg != 16:
            continue
        p.wait("tensor", "sc")
        emit_kv_assembly()

    p.barrier()
    for cm in (cm_ccsb, cm_kvt, cm_wq1, cm_work, cm_xtile, cm_wq0):
        cm.__exit__(None, None, None)

    kv_band, cm_kvband = sbuf("kv_band", [128, 64 * 256], BF16)
    kvidx, cm_kvidx = sbuf("kvidx_s", [128, 2 * 64], I32)
    def emit_band_gathers(st, wait_mm, jlo=None, jhi=64, sems=("gpk0", "gpk1")):
        if jlo is None:
            jlo = 32 if st == 0 else 0
        if wait_mm:
            p.wait("gpsimd", "mm")
        for j in range(jlo, jhi):
            p.op("gpsimd", lambda e, st=st, j=j: e.indirect_dma_start(
                kv_band[:, 256 * j:256 * (j + 1)], None,
                kv_cc_out.ap(),
                IndirectOffsetOnAxis(kvidx[:, st * 64 + j:st * 64 + j + 1], 0)),
                inc=sems[j % 2], incv=16)

    nkvi = p.op("sync", lambda e: e.dma_start(out=kvidx[:, :], in_=kvidx_in),
                inc="d_kvi", incv=16)
    p.wait("gpsimd", "cc")
    p.wait("gpsimd", "d_kvi", nkvi)
    emit_band_gathers(0, wait_mm=False)

    ktband, cm_ktband = sbuf("ktband", [128, 64 * 128], BF16)
    probs, cm_probs = sbuf("probs", [128, 4 * 512], BF16)
    trineg, cm_trineg = sbuf("trineg_s", [128, 4 * 512], BF16)
    biastab, cm_biastab = sbuf("biastab_s", [128, HEADS * 64], F32)
    sacc, cm_sacc = sbuf("sacc", [128, 1024], BF16)
    mskbuf, cm_mskbuf = sbuf("mskbuf", [128, 512], F32)
    dbase, cm_dbase = sbuf("dbase_s", [128, 512], F32)
    epi, cm_epi = sbuf("epi", [128, 512], F32)
    rsum, cm_rsum = sbuf("rsum", [1, 512], F32)

    p.op("sync", lambda e: e.dma_start(
        out=trineg[:, :].rearrange("p (m f) -> p m f", m=4),
        in_=trineg_in.rearrange("m p f -> p m f")), inc="d_in", incv=16)
    p.op("sync", lambda e: e.dma_start(out=dbase[:, :], in_=dbase_in),
         inc="d_in", incv=16)
    nmisc = p.cnt["d_in"]

    if DBG:
        p.wait("sync", "cc")
        p.op("sync", lambda e: e.dma_start(out=dbg_kv, in_=kv_cc_out.ap()),
             inc="d_out", incv=16)
    PSC = (PS[0], PS[1], PS[2], PS[3])
    PAT = (PS[4], PS[5])
    PSU6 = PS[6]
    PBC = PS[7]

    for st in range(2):
        if st == 1:
            p.wait("sync", "sc")
        nb = p.op("sync", lambda e, st=st: e.dma_start(
            out=biastab[:, :], in_=bias_in[:, st * HEADS * 64:(st + 1) * HEADS * 64]),
            inc="d_bias", incv=16)
        jmin = 32 if st == 0 else 0
        if st == 1:
            emit_band_gathers(1, wait_mm=True, jlo=32)
        p.wait("tensor", "gpk0")
        p.wait("tensor", "gpk1")
        if st == 1:
            p.wait("tensor", "gpe")
        if st == 1:
            p.wait("tensor", "ve")
        ktev = [None] * 64
        for j in range(jmin, 64):
            bank = PS[6 + j % 2]
            if j >= jmin + 2:
                p.wait("tensor", "ve", ktev[j - 2])
            ntr = p.op("tensor", lambda e, j=j, b=bank: e.matmul(
                b[:, 0:128], kv_band[:, 256 * j:256 * j + 128], ident[:, :],
                start=True, stop=True), inc="mm")
            p.wait("vector", "mm", ntr)
            ktev[j] = p.op("vector", lambda e, j=j, b=bank: e.tensor_copy(
                ktband[:, 128 * j:128 * (j + 1)], b[:, 0:128]), inc="ve")
        p.wait("tensor", "ve", ktev[63])
        if st == 0:
            emit_band_gathers(1, wait_mm=False, jlo=0, jhi=32,
                              sems=("gpe", "gpe"))
        p.wait("scalar", "d_bias", nb)
        p.wait("vector", "d_in", nmisc)

        for h in range(HEADS):
            L = min(L_H[h], 64 - jmin)
            j0 = 64 - L
            seedp = h < SEED_MAX
            at = PAT[h % 2]
            su = PSU6[0:1, :]
            if h >= 2:
                p.wait("tensor", "ve")
            exp_scale = SCALE
            ready = {}
            sacc_n = {}
            last_sacc = [0]

            def slot(j):
                return (j - j0) % 4

            def emit_scores(j):
                sl = slot(j)
                sc_b = PSC[sl]
                if j - 4 >= j0:
                    p.wait("tensor", ready[j - 4][0], ready[j - 4][1])
                nmm = p.op("tensor", lambda e, b=sc_b, j=j, h=h, st=st: e.matmul(
                    b[:, :], ktband[:, 128 * j:128 * (j + 1)],
                    bigB[:, h * 1024 + st * 512: h * 1024 + (st + 1) * 512],
                    start=True, stop=True), inc="mm")
                diag = j >= 60
                src_b = sc_b
                if seedp or diag:
                    p.wait("vector", "mm", nmm)
                    p.wait("vector", "sc")
                    if seedp:
                        nv = p.op("vector", lambda e, b=sc_b, h=h: e.scalar_tensor_tensor(
                            mskbuf[:, :], dbase[:, :], float(SLOPES[h]) / SCALE,
                            b[:, :], ALU.mult, ALU.add), inc="ve")
                        if diag:
                            p.wait("vector", "ve", nv)
                            nv = p.op("vector", lambda e, j=j: e.tensor_add(
                                mskbuf[:, :], mskbuf[:, :],
                                trineg[:, 512 * (j - 60):512 * (j - 59)]), inc="ve")
                    else:
                        nv = p.op("vector", lambda e, b=sc_b, j=j: e.tensor_add(
                            mskbuf[:, :], b[:, :],
                            trineg[:, 512 * (j - 60):512 * (j - 59)]), inc="ve")
                    p.wait("scalar", "ve", nv)
                    src_b = mskbuf
                else:
                    p.wait("scalar", "mm", nmm)
                if j - 4 >= j0 and ((j - 4 - j0) // 2) in sacc_n:
                    p.wait("scalar", "ve", sacc_n[(j - 4 - j0) // 2])
                nsc = p.op("scalar", lambda e, b=src_b, sl=sl, j=j, h=h, sc=exp_scale:
                           e.activation(probs[:, 512 * sl:512 * (sl + 1)], b[:, :],
                                        AF.Exp, bias=biastab[:, h * 64 + j:h * 64 + j + 1],
                                        scale=sc), inc="sc")
                ready[j] = ("sc", nsc)

            def emit_pv(j):
                sl = slot(j)
                p.wait("tensor", ready[j][0], ready[j][1])
                p.op("tensor", lambda e, a=at, j=j, sl=sl, first=(j == j0): e.matmul(
                    a[:, :], kv_band[:, 256 * j + 128:256 * (j + 1)],
                    probs[:, 512 * sl:512 * (sl + 1)],
                    start=first, stop=(j == 63)),
                    inc="mm" if j == 63 else None)

            def emit_sacc(m, ja, jb):
                p.wait("vector", ready[ja][0], ready[ja][1])
                if jb is not None:
                    p.wait("vector", ready[jb][0], ready[jb][1])
                sl = slot(ja)
                width = 1024 if jb is not None else 512
                if m == 0:
                    if width == 1024:
                        n = p.op("vector", lambda e, sl=sl: e.tensor_copy(
                            sacc[:, 0:1024], probs[:, 512 * sl:512 * sl + 1024]),
                            inc="ve")
                    else:
                        n = p.op("vector", lambda e, sl=sl: e.tensor_copy(
                            sacc[:, 0:512], probs[:, 512 * sl:512 * (sl + 1)]),
                            inc="ve")
                        nz = p.op("vector", lambda e: e.memset(sacc[:, 512:1024], 0.0),
                                  inc="ve")
                        n = nz
                else:
                    p.wait("vector", "ve", last_sacc[0])
                    if width == 1024:
                        n = p.op("vector", lambda e, sl=sl: e.tensor_add(
                            sacc[:, 0:1024], sacc[:, 0:1024],
                            probs[:, 512 * sl:512 * sl + 1024]), inc="ve")
                    else:
                        n = p.op("vector", lambda e, sl=sl: e.tensor_add(
                            sacc[:, 0:512], sacc[:, 0:512],
                            probs[:, 512 * sl:512 * (sl + 1)]), inc="ve")
                sacc_n[m] = n
                last_sacc[0] = n

            emit_scores(j0)
            if j0 + 1 <= 63:
                emit_scores(j0 + 1)
            for j in range(j0, 64):
                if j + 2 <= 63:
                    emit_scores(j + 2)
                emit_pv(j)
                off = j - j0
                if off % 2 == 1:
                    emit_sacc(off // 2, j - 1, j)
                elif j == 63:
                    emit_sacc(off // 2, j, None)
            p.wait("tensor", "ve", last_sacc[0])
            p.op("tensor", lambda e, s=su: e.matmul(
                s, onescb[:, :], sacc[:, 0:512], start=True, stop=False))
            nsum = p.op("tensor", lambda e, s=su: e.matmul(
                s, onescb[:, :], sacc[:, 512:1024], start=False, stop=True), inc="mm")
            p.wait("vector", "mm", nsum)
            nv = vch(lambda e, s=su: e.reciprocal(rsum[0:1, :], s))
            p.wait("tensor", "ve", nv)
            nmm2 = p.op("tensor", lambda e: e.matmul(
                PBC[:, :], onesr32[:, :], rsum[0:1, :], start=True, stop=True), inc="mm")
            p.wait("vector", "mm", nmm2)
            vch(lambda e: e.tensor_copy(epi[:, :], PBC[:, :]))
            nv = vch(lambda e, a=at, h=h, st=st: e.tensor_mul(
                bigA[:, h * 1024 + st * 512: h * 1024 + (st + 1) * 512],
                a[:, :], epi[:, :]))

    p.barrier()
    if DBG:
        p.op("sync", lambda e: e.dma_start(out=dbg_att, in_=bigA[:, :]),
             inc="d_out", incv=16)
    for cm in (cm_rsum, cm_epi, cm_dbase, cm_mskbuf, cm_sacc, cm_biastab,
               cm_trineg, cm_probs, cm_ktband):
        cm.__exit__(None, None, None)
    cm_kvidx.__exit__(None, None, None)
    cm_kvband.__exit__(None, None, None)

    sstage, cm_sstage = sbuf("sstage", [128, 512], BF16)

    p.op("gpsimd", lambda e: e.dma_start(
        out=bigB[:, :].rearrange("p (g c f) -> p g c f", g=2, c=32)[:, 0:1],
        in_=wo_in[0:1].rearrange("g c p f -> p g c f")),
        inc="d_wg0", incv=16)
    for oc in range(8):
        half = oc % 2
        wo_sl = (lambda h_, half=half: bigB[:, half * 16384 + h_ * 512:
                                            half * 16384 + (h_ + 1) * 512])
        p.wait("tensor", "d_wg0" if half == 0 else "d_wg1")
        if oc + 1 < 8:
            if oc >= 1:
                p.wait("gpsimd", "mm")
            p.op("gpsimd", lambda e, oc=oc, hf=1 - half: e.dma_start(
                out=bigB[:, :].rearrange("p (g c f) -> p g c f", g=2, c=32)[:, hf:hf + 1],
                in_=wo_in[oc + 1:oc + 2].rearrange("g c p f -> p g c f")),
                inc="d_wg0" if (1 - half) == 0 else "d_wg1", incv=16)
        for qs in range(8):
            bank = PS[qs]
            p.wait("tensor", "sc")
            nmm = None
            for h in range(32):
                nmm = p.op("tensor", lambda e, b=bank, h=h, qs=qs, wo_sl=wo_sl: e.matmul(
                    b[:, :], bigA[:, h * 1024 + qs * 128: h * 1024 + (qs + 1) * 128],
                    wo_sl(h), start=(h == 0), stop=(h == 31)),
                    inc="mm" if h == 31 else None)
            p.wait("scalar", "mm", nmm)
            ns = p.op("scalar", lambda e, b=bank: e.activation(
                sstage[:, :], b[:, :], AF.Copy), inc="sc")
            p.wait("sync", "sc", ns)
            p.op("sync", lambda e, oc=oc, qs=qs: e.dma_start(
                out=park_s[oc, 128 * qs:128 * (qs + 1), :], in_=sstage[:, :]),
                inc="d_park", incv=16)
            p.wait("scalar", "d_park")

    p.barrier()
    cm_sstage.__exit__(None, None, None)

    wff, cm_wff = sbuf("wff", [128, D], F32)
    ta, cm_ta = sbuf("ta", [128, D], BF16)
    tb, cm_tb = sbuf("tb", [128, D], BF16)
    wtpp, cm_wtpp = sbuf("wtpp", [128, 32 * 128], BF16)
    gstage = wff[:, :].bitcast(BF16)[:, 0:D]
    fo = bigA[:, :].bitcast(F32)[:, 0:D]
    fx = bigA[:, :].bitcast(F32)[:, D:2 * D]
    fs = bigA[:, :].bitcast(BF16)[:, 4 * D:6 * D]
    p.wait("sync", "d_park")
    p.op("sync", lambda e: e.dma_start(out=bigA[:, :], in_=park_xh), inc="d_park", incv=16)
    nxh = p.cnt["d_park"]

    nmm = p.op("tensor", lambda e: e.matmul(PS[6][0:1, 0:1], smalls[:, 16:17],
                                            onesc32[:, :], start=True, stop=True),
               inc="mm")
    p.wait("vector", "mm", nmm)
    vch(lambda e: e.tensor_copy(smalls[0:1, 18:19], PS[6][0:1, 0:1]))
    p.wait("tensor", "ve")
    nmm = p.op("tensor", lambda e: e.matmul(PS[6][:, 0:1], onesr32[:, :],
                                            smalls[0:1, 18:19], start=True, stop=True),
               inc="mm")
    p.wait("vector", "mm", nmm)
    vch(lambda e: e.tensor_scalar(smalls[:, 19:20], PS[6][:, 0:1],
                                  0.5 / (D * D), None, ALU.mult))
    vch(lambda e: e.tensor_scalar(smalls[:, 20:21], smalls[:, 19:20], -1.0, None,
                                  ALU.mult))
    vch(lambda e: e.tensor_scalar(smalls[:, 21:22], smalls[:, 19:20], 2.0, None,
                                  ALU.mult))

    HW = D // 2
    p.op("sync", lambda e: e.dma_start(out=wff[:, 0:HW], in_=wffn_in[0][:, 0:HW]),
         inc="d_ws0", incv=16)
    p.op("sync", lambda e: e.dma_start(out=wff[:, HW:D], in_=wffn_in[0][:, HW:D]),
         inc="d_ws1", incv=16)
    p.wait("tensor", "d_park", nxh)
    for oc in range(32):
        for hf in range(2):
            sl_ = slice(hf * HW, (hf + 1) * HW)
            p.wait("vector", "d_ws0" if hf == 0 else "d_ws1")
            vch(lambda e, sl_=sl_: e.tensor_scalar(
                tb[:, sl_], wff[:, sl_], smalls[:, 20:21], None, ALU.is_lt))
            nv = vch(lambda e, sl_=sl_: e.scalar_tensor_tensor(
                ta[:, sl_], wff[:, sl_], smalls[:, 19:20], tb[:, sl_],
                ALU.is_gt, ALU.subtract))
            if oc + 1 < 32:
                p.wait("sync", "ve", nv)
                p.op("sync", lambda e, oc=oc, sl_=sl_: e.dma_start(
                    out=wff[:, sl_], in_=wffn_in[oc + 1][:, sl_]),
                    inc="d_ws0" if hf == 0 else "d_ws1", incv=16)
        p.wait("tensor", "ve", nv)
        for grp in range(8):
            bank = PS[grp % 2]
            for q4 in range(4):
                ci = grp * 4 + q4
                p.op("tensor", lambda e, b=bank, q4=q4, ci=ci: e.matmul(
                    b[:, 128 * q4:128 * (q4 + 1)], ta[:, 128 * ci:128 * (ci + 1)],
                    ident[:, :], start=True, stop=True),
                    inc="mm" if q4 == 3 else None)
            p.wait("vector", "mm")
            nv2 = p.op("vector", lambda e, b=bank, grp=grp: e.tensor_copy(
                wtpp[:, 512 * grp:512 * (grp + 1)], b[:, :]), inc="ve")
            p.wait("tensor", "ve", nv2)
        for qh in range(2):
            bank = PS[2 + qh]
            p.wait("tensor", "sc")
            nmm = None
            for d in range(32):
                nmm = p.op("tensor", lambda e, b=bank, d=d, qh=qh: e.matmul(
                    b[:, :], wtpp[:, 128 * d:128 * (d + 1)],
                    bigA[:, d * 1024 + qh * 512: d * 1024 + (qh + 1) * 512],
                    start=(d == 0), stop=(d == 31)), inc="mm" if d == 31 else None)
            p.wait("scalar", "mm", nmm)
            p.op("scalar", lambda e, b=bank, oc=oc, qh=qh: e.activation(
                bigB[:, oc * 1024 + qh * 512: oc * 1024 + (qh + 1) * 512],
                b[:, :], AF.Copy), inc="sc")

    p.wait("tensor", "sc")
    fxn = [0] * NTILE
    fsn = [0] * NTILE
    stn = [0] * NTILE
    p.wait("sync", "d_park")
    p.wait("sync", "sc")
    fxn[0] = p.op("sync", lambda e: e.dma_start(out=fx, in_=x_in[0:128, :]),
                  inc="d_fx", incv=16)
    for t0 in range(2):
        fsn[t0] = p.op("sync", lambda e, t0=t0, b=t0 % 2: e.dma_start(
            out=fs[:, b * D:(b + 1) * D].rearrange("p (o f) -> p o f", o=8),
            in_=park_s[:, 128 * t0:128 * (t0 + 1), :].transpose([1, 0, 2])),
            inc="d_f0" if t0 % 2 == 0 else "d_f1", incv=16)
    for t in range(NTILE):
        if t >= 1:
            p.wait("scalar", "d_out", stn[t - 1])
        for grp in range(8):
            bank = PS[4 + grp % 2]
            for q4 in range(4):
                ci = grp * 4 + q4
                p.op("tensor", lambda e, b=bank, q4=q4, ci=ci, t=t: e.matmul(
                    b[:, 128 * q4:128 * (q4 + 1)], hbd[:, :],
                    bigB[:, ci * 1024 + 128 * t: ci * 1024 + 128 * (t + 1)],
                    start=True, stop=True), inc="mm" if q4 == 3 else None)
            eng, sem = ("vector", "ve") if grp % 2 == 0 else ("scalar", "sc")
            p.wait(eng, "mm")
            if eng == "vector":
                nv = p.op(eng, lambda e, b=bank, grp=grp: e.tensor_copy(
                    ta[:, 512 * grp:512 * (grp + 1)], b[:, :]), inc=sem)
            else:
                nv = p.op(eng, lambda e, b=bank, grp=grp: e.activation(
                    ta[:, 512 * grp:512 * (grp + 1)], b[:, :], AF.Copy), inc=sem)
            p.wait("tensor", sem, nv)
        tb_v = tb.ap().rearrange("p (j m i2) -> p m j i2", j=64, m=32, i2=2)
        for grp in range(8):
            bank = PS[6 + grp % 2]
            for q4 in range(4):
                ci = grp * 4 + q4
                p.op("tensor", lambda e, b=bank, q4=q4, ci=ci: e.matmul(
                    b[:, 128 * q4:128 * (q4 + 1)], ta[:, 128 * ci:128 * (ci + 1)],
                    ident[:, :], start=True, stop=True), inc="mm" if q4 == 3 else None)
            p.wait("vector", "mm")
            src = PS[6 + grp % 2].ap().rearrange(
                "p (m i2 j) -> p m j i2", m=4, i2=2, j=64)
            dst = tb_v[:, 4 * grp:4 * (grp + 1)]
            nv = p.op("vector", lambda e, s=src, d=dst: e.tensor_copy(d, s), inc="ve")
            p.wait("tensor", "ve", nv)
        for grp in range(8):
            bank = PS[0 + grp % 2]
            for q4 in range(4):
                ci = grp * 4 + q4
                p.op("tensor", lambda e, b=bank, q4=q4, ci=ci: e.matmul(
                    b[:, 128 * q4:128 * (q4 + 1)], tb[:, 128 * ci:128 * (ci + 1)],
                    ident[:, :], start=True, stop=True), inc="mm" if q4 == 3 else None)
            eng, sem = ("vector", "ve") if grp % 2 == 0 else ("scalar", "sc")
            p.wait(eng, "mm")
            if eng == "vector":
                nv = p.op(eng, lambda e, b=bank, grp=grp: e.tensor_copy(
                    ta[:, 512 * grp:512 * (grp + 1)], b[:, :]), inc=sem)
            else:
                nv = p.op(eng, lambda e, b=bank, grp=grp: e.activation(
                    ta[:, 512 * grp:512 * (grp + 1)], b[:, :], AF.Copy), inc=sem)
            p.wait("tensor", sem, nv)
        gs_v = gstage.rearrange("p (i g m o2) -> p g m i o2", i=64, g=8, m=4, o2=2)
        for grp in range(8):
            bank = PS[2 + grp % 2]
            for q4 in range(4):
                ci = grp * 4 + q4
                p.op("tensor", lambda e, b=bank, q4=q4, ci=ci: e.matmul(
                    b[:, 128 * q4:128 * (q4 + 1)], ta[:, 128 * ci:128 * (ci + 1)],
                    hbd[:, :], start=True, stop=True), inc="mm" if q4 == 3 else None)
            p.wait("scalar", "mm")
            src = PS[2 + grp % 2].ap().rearrange(
                "p (g m o2 i) -> p g m i o2", g=1, m=4, o2=2, i=64)
            dst = gs_v[:, grp:grp + 1]
            ns = p.op("scalar", lambda e, s=src, d=dst: e.activation(
                d, s, AF.Silu, scale=smalls[:, 21:22]), inc="sc")
            p.wait("tensor", "sc", ns)
        p.wait("vector", "d_f0" if t % 2 == 0 else "d_f1", fsn[t])
        p.wait("vector", "d_fx", fxn[t])
        p.wait("vector", "sc")
        if t >= 1:
            p.wait("vector", "d_out", stn[t - 1])
        vch(lambda e, b=t % 2: e.tensor_mul(fo, fs[:, b * D:(b + 1) * D], gstage))
        nv = vch(lambda e: e.tensor_add(fo, fo, fx))
        p.wait("sync", "ve", nv)
        stn[t] = p.op("sync", lambda e, t=t: e.dma_start(
            out=out_d[128 * t:128 * (t + 1), :], in_=fo), inc="d_out", incv=16)
        if t + 1 < NTILE:
            fxn[t + 1] = p.op("sync", lambda e, t=t: e.dma_start(
                out=fx, in_=x_in[128 * (t + 1):128 * (t + 2), :]),
                inc="d_fx", incv=16)
        if t + 2 < NTILE:
            fsn[t + 2] = p.op("sync", lambda e, t=t, b=t % 2: e.dma_start(
                out=fs[:, b * D:(b + 1) * D].rearrange("p (o f) -> p o f", o=8),
                in_=park_s[:, 128 * (t + 2):128 * (t + 3), :].transpose([1, 0, 2])),
                inc="d_f0" if t % 2 == 0 else "d_f1", incv=16)

    for cm in (cm_wtpp, cm_tb, cm_ta, cm_wff):
        cm.__exit__(None, None, None)

    p.wait("sync", "d_out")

    blk_cm = nc.Block()
    blk = blk_cm.__enter__()
    for name in ENGS:
        thunks = p.stream[name]
        getattr(blk, name)(lambda e, th=thunks: [f(e) for f in th])
    blk_cm.__exit__(None, None, None)

    for cm in reversed(ctx):
        cm.__exit__(None, None, None)
    return nc



def _h64():
    m = np.array([[1.0]])
    for _ in range(6):
        m = np.block([[m, m], [m, -m]])
    return m


def _host_constants():
    H64 = _h64()
    hbd = np.zeros((128, 128), np.float32)
    hbd[:64, :64] = H64
    hbd[64:, 64:] = H64
    ident = np.eye(128, dtype=np.float32)
    pk = np.arange(128)[:, None]
    fq = np.arange(512)[None, :]
    trineg = np.stack([np.where(j2 * 128 + pk <= fq, 0.0, NEG).astype(np.float32)
                       for j2 in range(4)])
    pkf = np.arange(128, dtype=np.float64)[:, None] - np.arange(512, dtype=np.float64)[None, :]
    return dict(
        dbase=pkf.astype(np.float32),
        hbd=hbd.astype(BF), ident=ident.astype(BF),
        trineg=trineg.astype(BF),
        ones_col_bf=np.ones((128, 1), BF), ones_col_f32=np.ones((128, 1), np.float32),
        ones_row_f32=np.ones((1, 128), np.float32),
    )


def _core_tables(c):
    stripes = (c, 15 - c)
    bias = np.zeros((128, 2, HEADS, 64), np.float64)
    kvidx = np.zeros((128, 2, 64), np.int64)
    pk = np.arange(128)
    for st, sidx in enumerate(stripes):
        E = 4 * sidx + 3
        q0 = 512 * sidx
        for j in range(64):
            ktg = E - 63 + j
            if ktg < 0:
                kvidx[:, st, j] = 0
                bias[:, st, :, j] = NEG
                continue
            g = 128 * ktg + pk
            srow = g // 512
            r = np.where(srow < 8, srow, 15 - srow)
            local = np.where(srow < 8, g - 512 * r, 512 + (g - 512 * (15 - r)))
            kvidx[:, st, j] = 1024 * r + local
            for h in range(HEADS):
                s = float(SLOPES[h])
                if h < SEED_MAX:
                    bias[:, st, h, j] = s * (128.0 * ktg - q0)
                else:
                    bias[:, st, h, j] = s * (128.0 * ktg + pk - (q0 + 511.0))
    return (bias.reshape(128, 2 * HEADS * 64).astype(np.float32),
            kvidx.reshape(128, 2 * 64).astype(np.int32))


DPERM = np.array([(dpp % 64) * 64 + dpp // 64 for dpp in range(D)])

_NC_CACHE = {}


def kernel(L0_attn_hidden_states, L0_attn_Wqkv, L0_attn_Wo, L0_ffn_gamma,
           L0_ffn_beta, L0_ffn_weight):
    x = np.ascontiguousarray(np.asarray(L0_attn_hidden_states, dtype=np.float32)
                             .reshape(S, D))
    wqkv = np.asarray(L0_attn_Wqkv, dtype=np.float32)
    wo = np.asarray(L0_attn_Wo, dtype=np.float32)
    gamma = np.asarray(L0_ffn_gamma, dtype=np.float32)
    beta = np.asarray(L0_ffn_beta, dtype=np.float32)
    wffn = np.asarray(L0_ffn_weight, dtype=np.float32)

    wqkv_t = np.ascontiguousarray(
        wqkv.reshape(32, 128, 34, 128).transpose(2, 0, 1, 3))
    wo_t = np.ascontiguousarray(
        wo.reshape(32, 128, 8, 512).transpose(2, 0, 1, 3))
    wffn_t = np.ascontiguousarray(
        wffn[:, DPERM].reshape(32, 128, D))
    gamma_b = np.ascontiguousarray(np.broadcast_to(gamma[None, :], (128, D)))
    beta_b = np.ascontiguousarray(np.broadcast_to(beta[None, :], (128, D)))

    consts = _host_constants()

    in_maps = []
    for c in range(NCORE):
        rows = np.r_[512 * c:512 * c + 512, 512 * (15 - c):512 * (15 - c) + 512]
        bias_tab, kv_idx = _core_tables(c)
        m = dict(
            x_sh=np.ascontiguousarray(x[rows]),
            wqkv_t=wqkv_t, wo_t=wo_t, wffn_t=wffn_t,
            gamma_b=gamma_b, beta_b=beta_b,
            bias_tab=bias_tab, kv_idx=kv_idx,
            dbase=consts["dbase"], hbd=consts["hbd"], ident=consts["ident"],
            trineg=consts["trineg"],
            ones_col_bf=consts["ones_col_bf"], ones_col_f32=consts["ones_col_f32"],
            ones_row_f32=consts["ones_row_f32"],
        )
        in_maps.append(m)

    if "nc" not in _NC_CACHE:
        _NC_CACHE["nc"] = build_program()
    nc = _NC_CACHE["nc"]

    res = run_bass_kernel_spmd(nc, in_maps, core_ids=list(range(NCORE)))

    out = np.empty((S, D), np.float32)
    for c in range(NCORE):
        o = res.results[c]["out_sh"]
        out[512 * c:512 * c + 512] = o[0:512]
        out[512 * (15 - c):512 * (15 - c) + 512] = o[512:1024]
    return out.reshape(1, S, D)


if __name__ == "__main__":
    nc = build_program()
    print("build ok")


# version 14
# speedup vs baseline: 1.0031x; 1.0031x over previous
import math
import numpy as np
import ml_dtypes

import os
import concourse.bass as bass
import concourse.mybir as mybir
from concourse.bass_utils import run_bass_kernel_spmd
from concourse.bass_core_rust import IndirectOffsetOnAxis

F32 = mybir.dt.float32
BF16 = mybir.dt.bfloat16
I32 = mybir.dt.int32
AF = mybir.ActivationFunctionType
ALU = mybir.AluOpType
AX = mybir.AxisListType

NCORE = 8
S = 8192
D = 4096
HEADS = 32
HD = 128
ROWS = 1024
NTILE = 8
SCALE = float(HD) ** -0.5
EPS = 1e-5
T_CUT = 80.0
SEED_MAX = 10
NEG = -1e30
BF = ml_dtypes.bfloat16


def _slopes():
    base = 2.0 ** (-(2.0 ** -(math.log2(HEADS) - 3)))
    return (base ** np.arange(1, HEADS + 1, dtype=np.float64)).astype(np.float32)


SLOPES = _slopes()
L_H = [min(4 + int(math.ceil((T_CUT / float(SLOPES[h])) / 128.0)), 64)
       for h in range(HEADS)]


ENGS = ("tensor", "scalar", "vector", "sync", "gpsimd")


class Plan:
    def __init__(self, nc):
        self.nc = nc
        self.stream = {e: [] for e in ENGS}
        self.sems = {}
        self.cnt = {}
        self._ctx = []

    def sem(self, name):
        cm = self.nc.semaphore(name)
        h = cm.__enter__()
        self._ctx.append(cm)
        self.sems[name] = h
        self.cnt[name] = 0

    def op(self, eng, fn, inc=None, incv=1):
        if inc is None:
            self.stream[eng].append(lambda e, fn=fn: fn(e))
            return None
        h = self.sems[inc]
        if incv == "dflt":
            self.cnt[inc] += 1
            self.stream[eng].append(lambda e, fn=fn, h=h: fn(e).then_inc(h))
        else:
            self.cnt[inc] += incv
            self.stream[eng].append(
                lambda e, fn=fn, h=h, v=incv: fn(e).then_inc(h, v))
        return self.cnt[inc]

    def wait(self, eng, name, val=None):
        v = self.cnt[name] if val is None else val
        if v <= 0:
            return
        h = self.sems[name]
        self.stream[eng].append(lambda e, h=h, v=v: e.wait_ge(h, v))

    def barrier(self, name="ph"):
        for e in ENGS:
            self.op(e, lambda en: en.nop(), inc=name)
        tgt = self.cnt[name]
        for e in ENGS:
            self.wait(e, name, tgt)
            for d in ("d_park", "d_out", "d_kv"):
                self.wait(e, d)


def build_program():
    nc = bass.Bass("TRN2", target_bir_lowering=False)
    p = Plan(nc)
    ctx = p._ctx

    def sbuf(name, shape, dt):
        cm = nc.sbuf_tensor(name, shape, dt)
        h = cm.__enter__()
        return h, cm

    def psum(name, shape):
        cm = nc.psum_tensor(name, shape, F32)
        h = cm.__enter__()
        ctx.append(cm)
        return h

    x_in = nc.dram_tensor("x_sh", [ROWS, D], F32, kind="ExternalInput").ap()
    wqkv_in = nc.dram_tensor("wqkv_t", [34, 32, 128, 128], F32, kind="ExternalInput").ap()
    wo_in = nc.dram_tensor("wo_t", [8, 32, 128, 512], F32, kind="ExternalInput").ap()
    wffn_in = nc.dram_tensor("wffn_t", [32, 128, D], F32, kind="ExternalInput").ap()
    gamma_in = nc.dram_tensor("gamma_b", [128, D], F32, kind="ExternalInput").ap()
    beta_in = nc.dram_tensor("beta_b", [128, D], F32, kind="ExternalInput").ap()
    bias_in = nc.dram_tensor("bias_tab", [128, 2 * HEADS * 64], F32, kind="ExternalInput").ap()
    kvidx_in = nc.dram_tensor("kv_idx", [128, 2 * 64], I32, kind="ExternalInput").ap()
    hbd_in = nc.dram_tensor("hbd", [128, 128], BF16, kind="ExternalInput").ap()
    ident_in = nc.dram_tensor("ident", [128, 128], BF16, kind="ExternalInput").ap()
    trineg_in = nc.dram_tensor("trineg", [4, 128, 512], BF16, kind="ExternalInput").ap()
    dbase_in = nc.dram_tensor("dbase", [128, 512], F32, kind="ExternalInput").ap()
    onescb_in = nc.dram_tensor("ones_col_bf", [128, 1], BF16, kind="ExternalInput").ap()
    onesc32_in = nc.dram_tensor("ones_col_f32", [128, 1], F32, kind="ExternalInput").ap()
    onesr32_in = nc.dram_tensor("ones_row_f32", [1, 128], F32, kind="ExternalInput").ap()
    out_d = nc.dram_tensor("out_sh", [ROWS, D], F32, kind="ExternalOutput").ap()

    DBG = os.environ.get("KDEBUG", "") == "1"
    dk = dict(kind="ExternalOutput") if DBG else {}
    park_xh = nc.dram_tensor("park_xh", [128, 32 * 1024], BF16, **dk).ap()
    park_s = nc.dram_tensor("park_s", [8, ROWS, 512], BF16, **dk).ap()
    if DBG:
        dbg_xnt = nc.dram_tensor("dbg_xnt", [128, 32 * 1024], BF16, kind="ExternalOutput").ap()
        dbg_kv = nc.dram_tensor("dbg_kv", [S, 256], BF16, kind="ExternalOutput").ap()
        dbg_att = nc.dram_tensor("dbg_att", [128, 32 * 1024], BF16, kind="ExternalOutput").ap()
    kv_cc_in = nc.dram_tensor("kv_cc_in", [ROWS, 256], BF16)
    kv_cc_out = nc.dram_tensor("kv_cc_out", [S, 256], BF16)

    for s_ in ("d_in", "d_w", "d_wg0", "d_wg1", "d_kv", "d_park", "d_out", "cc",
               "mm", "sc", "ve", "gp", "gpk0", "gpk1", "d_bias", "d_ws0", "d_ws1",
               "d_f0", "d_f1", "d_fx", "d_kvi", "gpe", "ph"):
        p.sem(s_)

    bigA, _ = sbuf("bigA", [128, 32 * 1024], BF16)
    bigB, _ = sbuf("bigB", [128, 32 * 1024], BF16)
    hbd, _ = sbuf("hbd_s", [128, 128], BF16)
    ident, _ = sbuf("ident_s", [128, 128], BF16)
    onescb, _ = sbuf("onescb_s", [128, 1], BF16)
    onesc32, _ = sbuf("onesc32_s", [128, 1], F32)
    onesr32, _ = sbuf("onesr32_s", [1, 128], F32)
    stats, _ = sbuf("stats", [128, 64], F32)
    smalls, _ = sbuf("smalls", [128, 24], F32)
    wq0, cm_wq0 = sbuf("wq0", [128, 2 * 32 * 128], BF16)
    PS = [psum(f"ps{i}", [128, 512]) for i in range(8)]

    p.op("gpsimd", lambda e: e.dma_start(
        out=wq0[:, :].rearrange("p (c d f) -> p c d f", c=2, d=32),
        in_=wqkv_in[32:34].rearrange("c d p f -> p c d f")),
        inc="d_wg0", incv=16)

    xtile, cm_xtile = sbuf("xtile", [128, D], F32)
    work, cm_work = sbuf("work", [128, D], F32)
    work2, cm_work2 = sbuf("work2", [128, D], F32)
    wkb, cm_wkb = sbuf("wkb", [128, D], BF16)
    wkb2 = bigB[:, 16 * 1024:20 * 1024]
    tmpa = work[:, :].bitcast(BF16)[:, 0:D]
    tmpb = work2[:, :].bitcast(BF16)[:, 0:D]
    gb_sb = bigB[:, :].bitcast(F32)[:, 0:D]
    bb_sb = bigB[:, :].bitcast(F32)[:, D:2 * D]

    for dst, src in [(hbd, hbd_in), (ident, ident_in),
                     (onescb, onescb_in), (onesc32, onesc32_in),
                     (onesr32, onesr32_in), (gb_sb, gamma_in), (bb_sb, beta_in)]:
        p.op("sync", lambda e, d=dst, s=src: e.dma_start(out=d[:, :], in_=s),
             inc="d_in", incv=16)
    p.op("sync", lambda e: e.dma_start(out=xtile[:, :], in_=x_in[0:128, :]),
         inc="d_in", incv=16)
    n_const = p.cnt["d_in"]
    p.wait("tensor", "d_in", n_const)
    p.wait("scalar", "d_in", n_const)

    def vch(fn):
        n = p.op("vector", fn, inc="ve")
        p.wait("vector", "ve", n)
        return n

    for t in range(NTILE):
        p.wait("vector", "d_in", n_const + 16 * t)
        if t >= 1:
            p.wait("vector", "d_park")
        for k in range(8):
            if k < 7:
                p.op("vector", lambda e, k=k: e.bn_stats(
                    stats[:, 6 * k:6 * k + 6], xtile[:, 512 * k:512 * (k + 1)]))
            else:
                vch(lambda e, k=k: e.bn_stats(
                    stats[:, 6 * k:6 * k + 6], xtile[:, 512 * k:512 * (k + 1)]))
        vch(lambda e: e.bn_aggr(smalls[:, 0:2], stats[:, 0:48]))
        vch(lambda e: e.tensor_scalar(smalls[:, 2:3], smalls[:, 1:2], EPS, None, ALU.add))
        p.wait("scalar", "ve")
        n = p.op("scalar", lambda e: e.activation(smalls[:, 3:4], smalls[:, 2:3], AF.Sqrt),
                 inc="sc")
        p.wait("vector", "sc", n)
        vch(lambda e: e.reciprocal(smalls[:, 4:5], smalls[:, 3:4]))
        vch(lambda e: e.tensor_mul(smalls[:, 5:6], smalls[:, 0:1], smalls[:, 4:5]))
        vch(lambda e: e.tensor_scalar(smalls[:, 6:7], smalls[:, 5:6], -1.0, None, ALU.mult))
        p.wait("scalar", "ve")
        n = p.op("scalar", lambda e: e.activation(
            work[:, :], xtile[:, :], AF.Identity, bias=smalls[:, 6:7], scale=smalls[:, 4:5]),
            inc="sc")
        p.wait("vector", "sc", n)
        nwkb = vch(lambda e: e.tensor_copy(wkb[:, :], work[:, :]))
        if t + 1 < NTILE:
            p.wait("sync", "sc", n)
            p.op("sync", lambda e, t=t: e.dma_start(
                out=xtile[:, :], in_=x_in[128 * (t + 1):128 * (t + 2), :]),
                inc="d_in", incv=16)
        for k in range(8):
            if k < 7:
                p.op("vector", lambda e, k=k: e.bn_stats(
                    stats[:, 6 * k:6 * k + 6], work[:, 512 * k:512 * (k + 1)]))
            else:
                vch(lambda e, k=k: e.bn_stats(
                    stats[:, 6 * k:6 * k + 6], work[:, 512 * k:512 * (k + 1)]))
        vch(lambda e: e.bn_aggr(smalls[:, 8:10], stats[:, 0:48]))
        vch(lambda e: e.tensor_scalar(smalls[:, 10:11], smalls[:, 9:10], EPS, None, ALU.add))
        p.wait("scalar", "ve")
        n = p.op("scalar", lambda e: e.activation(smalls[:, 11:12], smalls[:, 10:11], AF.Sqrt),
                 inc="sc")
        p.wait("vector", "sc", n)
        vch(lambda e: e.reciprocal(smalls[:, 12:13], smalls[:, 11:12]))
        vch(lambda e: e.tensor_mul(smalls[:, 13:14], smalls[:, 8:9], smalls[:, 12:13]))
        vch(lambda e: e.tensor_scalar(smalls[:, 14:15], smalls[:, 13:14], -1.0, None, ALU.mult))
        p.wait("scalar", "ve")
        n = p.op("scalar", lambda e: e.activation(
            work2[:, :], work[:, :], AF.Identity, bias=smalls[:, 14:15], scale=smalls[:, 12:13]),
            inc="sc")
        p.wait("vector", "sc", n)
        vch(lambda e: e.tensor_mul(work[:, :], work2[:, :], gb_sb))
        vch(lambda e: e.tensor_add(work2[:, :], work[:, :], bb_sb))
        vch(lambda e: e.tensor_reduce(smalls[:, 7:8], work2[:, :], AX.X, ALU.max,
                                      apply_absolute_value=True))
        vch(lambda e: e.tensor_scalar(smalls[:, 7:8], smalls[:, 7:8], 1e-6, 1e6,
                                      ALU.max, ALU.min))
        vch(lambda e: e.reciprocal(smalls[:, 7:8], smalls[:, 7:8]))
        vch(lambda e: e.tensor_scalar(smalls[:, 7:8], smalls[:, 7:8], 7.0, None, ALU.mult))
        vch(lambda e: e.tensor_scalar(smalls[:, 7:8], smalls[:, 7:8], 1e-6, 1e6,
                                      ALU.max, ALU.min))
        p.wait("scalar", "ve")
        n = p.op("scalar", lambda e: e.activation(
            work[:, :], work2[:, :], AF.Identity, scale=smalls[:, 7:8]), inc="sc")
        p.wait("vector", "sc", n)
        vch(lambda e: e.tensor_copy(work2[:, :].bitcast(I32), work[:, :]))
        vch(lambda e: e.tensor_copy(wkb2[:, :], work2[:, :].bitcast(I32)))

        def stage(src_get, dst_put, kind, banks, dst_batched=None):
            evn = [None] * 8
            for grp in range(8):
                bank = PS[banks[grp % 2]]
                if grp >= 2:
                    p.wait("tensor", evn[grp - 2][0], evn[grp - 2][1])
                for q4 in range(4):
                    ci = grp * 4 + q4
                    if kind == "tr":
                        p.op("tensor", lambda e, b=bank, q4=q4, ci=ci: e.matmul(
                            b[:, 128 * q4:128 * (q4 + 1)], src_get(ci), ident[:, :],
                            start=True, stop=True),
                            inc="mm" if q4 == 3 else None)
                    else:
                        p.op("tensor", lambda e, b=bank, q4=q4, ci=ci: e.matmul(
                            b[:, 128 * q4:128 * (q4 + 1)], hbd[:, :], src_get(ci),
                            start=True, stop=True),
                            inc="mm" if q4 == 3 else None)
                if dst_batched is not None:
                    eng = "vector" if grp % 2 == 0 else "scalar"
                    sem = "ve" if eng == "vector" else "sc"
                    p.wait(eng, "mm")
                    if eng == "vector":
                        n = p.op(eng, lambda e, b=bank, grp=grp: e.tensor_copy(
                            dst_batched(grp), b[:, :]), inc=sem)
                    else:
                        n = p.op(eng, lambda e, b=bank, grp=grp: e.activation(
                            dst_batched(grp), b[:, :], AF.Copy), inc=sem)
                    evn[grp] = (sem, n)
                else:
                    p.wait("vector", "mm")
                    nv = None
                    for q4 in range(4):
                        ci = grp * 4 + q4
                        nv = p.op("vector", lambda e, b=bank, q4=q4, ci=ci: e.tensor_copy(
                            dst_put(ci), b[:, 128 * q4:128 * (q4 + 1)]),
                            inc="ve" if q4 == 3 else None)
                    evn[grp] = ("ve", nv)
            p.wait("tensor", evn[6][0], evn[6][1])
            p.wait("tensor", evn[7][0], evn[7][1])

        p.wait("tensor", "ve", nwkb)
        bigA_v = bigA[:, :].rearrange("p (c q) -> p c q", c=32)
        stage(lambda ci: wkb[:, 128 * ci:128 * (ci + 1)], None, "tr", (0, 1),
              dst_batched=lambda grp, t=t: bigA_v[:, 4 * grp:4 * (grp + 1),
                                                  128 * t:128 * (t + 1)])
        stage(lambda ci: wkb2[:, 128 * ci:128 * (ci + 1)], None, "tr", (2, 3),
              dst_batched=lambda grp: tmpa[:, 512 * grp:512 * (grp + 1)])
        stage(lambda ci: tmpa[:, 128 * ci:128 * (ci + 1)], None, "hbd", (4, 5),
              dst_batched=lambda grp: tmpb[:, 512 * grp:512 * (grp + 1)])
        wkb_v = wkb.ap().rearrange("p (j m i2) -> p m j i2", j=64, m=32, i2=2)
        evn = [None] * 8
        for grp in range(8):
            bank = PS[6 + grp % 2]
            if grp >= 2:
                p.wait("tensor", "ve", evn[grp - 2])
            for q4 in range(4):
                ci = grp * 4 + q4
                p.op("tensor", lambda e, b=bank, q4=q4, ci=ci: e.matmul(
                    b[:, 128 * q4:128 * (q4 + 1)],
                    tmpb[:, 128 * ci:128 * (ci + 1)], ident[:, :],
                    start=True, stop=True), inc="mm" if q4 == 3 else None)
            p.wait("vector", "mm")
            nv = None
            for q4 in range(4):
                ci = grp * 4 + q4
                src = PS[6 + grp % 2].ap().rearrange(
                    "p (m i2 j) -> p m j i2", m=4, i2=2, j=64)[:, q4:q4 + 1]
                dst = wkb_v[:, ci:ci + 1]
                nv = p.op("vector", lambda e, s=src, d=dst: e.tensor_copy(d, s),
                          inc="ve" if q4 == 3 else None)
            evn[grp] = nv
        p.wait("tensor", "ve", evn[7])
        stage(lambda ci: wkb[:, 128 * ci:128 * (ci + 1)], None, "tr", (0, 1),
              dst_batched=lambda grp: tmpa[:, 512 * grp:512 * (grp + 1)])
        stage(lambda ci: tmpa[:, 128 * ci:128 * (ci + 1)], None, "hbd", (2, 3),
              dst_batched=lambda grp: wkb2[:, 512 * grp:512 * (grp + 1)])
        p.wait("sync", "ve")
        p.wait("sync", "sc")
        for q8 in range(4):
            p.op("sync", lambda e, t=t, q8=q8: e.dma_start(
                out=park_xh[:, :].rearrange("p (g c q) -> p g c q", g=1, c=32)[
                    :, :, 8 * q8:8 * (q8 + 1), 128 * t:128 * (t + 1)],
                in_=bigB[:, :].rearrange("p (g c q) -> p g c q", g=8, c=32)[
                    :, 4:5, 8 * q8:8 * (q8 + 1), :]),
                inc="d_park", incv=16)

    if DBG:
        p.wait("sync", "ve")
        p.op("sync", lambda e: e.dma_start(out=dbg_xnt, in_=bigA[:, :]),
             inc="d_out", incv=16)
    p.barrier()
    for cm in (cm_wkb, cm_work2):
        cm.__exit__(None, None, None)

    wq1, cm_wq1 = sbuf("wq1", [128, 2 * 32 * 128], BF16)
    kvt, cm_kvt = sbuf("kvt", [128, 2 * 1024], BF16)
    ccsb, cm_ccsb = sbuf("ccsb", [128, 8 * 256], BF16)
    WQ = (wq0, wq1)

    WSC = (xtile, work)
    p.op("vector", lambda e: e.memset(smalls[:, 16:17], 0.0), inc="ve")
    p.op("sync", lambda e: e.dma_start(out=xtile[:, :], in_=wffn_in[0]),
         inc="d_ws0", incv=16)
    scred = [0, 0]
    for ci in range(32):
        par = ci % 2
        if ci + 1 < 32:
            if ci >= 1:
                p.wait("sync", "ve", scred[(ci + 1) % 2])
            p.op("sync", lambda e, ci=ci, b=WSC[(ci + 1) % 2]: e.dma_start(
                out=b[:, :], in_=wffn_in[ci + 1]),
                inc="d_ws0" if (ci + 1) % 2 == 0 else "d_ws1", incv=16)
        p.wait("vector", "d_ws0" if par == 0 else "d_ws1")
        p.op("vector", lambda e, b=WSC[par]: e.tensor_reduce(
            smalls[:, 17:18], b[:, :], AX.X, ALU.add, apply_absolute_value=True),
            inc="ve")
        p.wait("vector", "ve")
        scred[par] = p.op("vector", lambda e: e.tensor_add(
            smalls[:, 16:17], smalls[:, 16:17], smalls[:, 17:18]), inc="ve")
        p.wait("vector", "ve", scred[par])

    def qkv_scale(col):
        if col < HEADS and col < SEED_MAX:
            return SCALE / float(SLOPES[col])
        return 1.0

    ev_hist = []
    p.wait("scalar", "d_park")
    def emit_kv_assembly():
        for feat in range(2):
            for tt in range(8):
                bank = PS[4 + (feat * 8 + tt) % 2]
                nmm = p.op("tensor", lambda e, b=bank, feat=feat, tt=tt: e.matmul(
                    b[:, 0:128], kvt[:, feat * 1024 + 128 * tt: feat * 1024 + 128 * (tt + 1)],
                    ident[:, :], start=True, stop=True), inc="mm")
                p.wait("vector", "mm", nmm)
                nv = p.op("vector", lambda e, b=bank, feat=feat, tt=tt: e.tensor_copy(
                    ccsb[:, 256 * tt + 128 * feat: 256 * tt + 128 * (feat + 1)],
                    b[:, 0:128]), inc="ve")
                p.wait("tensor", "ve", nv)
        p.wait("sync", "ve")
        p.op("sync", lambda e: e.dma_start(
            out=kv_cc_in.ap().rearrange("(t p) f -> p t f", p=128),
            in_=ccsb[:, :].rearrange("p (t f) -> p t f", t=8)), inc="d_kv", incv=16)
        p.wait("gpsimd", "d_kv")
        p.op("gpsimd", lambda e: e.collective_compute(
            "AllGather", ALU.bypass, replica_groups=[list(range(NCORE))],
            ins=[kv_cc_in.ap().opt()], outs=[kv_cc_out.ap().opt()]), inc="cc",
            incv="dflt")

    CGORD = [16] + list(range(16))
    for cgi in range(17):
        cg = CGORD[cgi]
        wbuf = WQ[cgi % 2]
        p.wait("tensor", "d_wg0" if cgi % 2 == 0 else "d_wg1")
        if cgi + 1 < 17:
            nbuf = WQ[(cgi + 1) % 2]
            ncg = CGORD[cgi + 1]
            if cgi == 0:
                p.wait("gpsimd", "d_park")
            if cgi >= 1:
                p.wait("gpsimd", "mm", ev_hist[cgi - 1])
            p.op("gpsimd", lambda e, ncg=ncg, nb=nbuf: e.dma_start(
                out=nb[:, :].rearrange("p (c d f) -> p c d f", c=2, d=32),
                in_=wqkv_in[2 * ncg:2 * ncg + 2].rearrange("c d p f -> p c d f")),
                inc="d_wg0" if (cgi + 1) % 2 == 0 else "d_wg1", incv=16)
        for cc in range(2):
            col = 2 * cg + cc
            for qh in range(2):
                bank = PS[2 * cc + qh]
                p.wait("tensor", "sc")
                nmm = None
                for d in range(32):
                    nmm = p.op("tensor", lambda e, b=bank, cc=cc, d=d, qh=qh,
                               w=wbuf: e.matmul(
                        b[:, :], w[:, (cc * 32 + d) * 128:(cc * 32 + d + 1) * 128],
                        bigA[:, d * 1024 + qh * 512: d * 1024 + (qh + 1) * 512],
                        start=(d == 0), stop=(d == 31)),
                        inc="mm" if d == 31 else None)
                p.wait("scalar", "mm", nmm)
                if col < HEADS:
                    p.op("scalar", lambda e, b=bank, col=col, qh=qh: e.activation(
                        bigB[:, col * 1024 + qh * 512: col * 1024 + (qh + 1) * 512],
                        b[:, :], AF.Copy), inc="sc")
                else:
                    p.op("scalar", lambda e, b=bank, col=col, qh=qh: e.activation(
                        kvt[:, (col - 32) * 1024 + qh * 512:(col - 32) * 1024 + (qh + 1) * 512],
                        b[:, :], AF.Copy), inc="sc")
        ev_hist.append(p.cnt["mm"])
        if cg != 16:
            continue
        p.wait("tensor", "sc")
        emit_kv_assembly()

    p.barrier()
    for cm in (cm_ccsb, cm_kvt, cm_wq1, cm_work, cm_xtile, cm_wq0):
        cm.__exit__(None, None, None)

    kv_band, cm_kvband = sbuf("kv_band", [128, 64 * 256], BF16)
    kvidx, cm_kvidx = sbuf("kvidx_s", [128, 2 * 64], I32)
    def emit_band_gathers(st, wait_mm, jlo=None, jhi=64, sems=("gpk0", "gpk1")):
        if jlo is None:
            jlo = 32 if st == 0 else 0
        if wait_mm:
            p.wait("gpsimd", "mm")
        for j in range(jlo, jhi):
            p.op("gpsimd", lambda e, st=st, j=j: e.indirect_dma_start(
                kv_band[:, 256 * j:256 * (j + 1)], None,
                kv_cc_out.ap(),
                IndirectOffsetOnAxis(kvidx[:, st * 64 + j:st * 64 + j + 1], 0)),
                inc=sems[j % 2], incv=16)

    nkvi = p.op("sync", lambda e: e.dma_start(out=kvidx[:, :], in_=kvidx_in),
                inc="d_kvi", incv=16)
    p.wait("gpsimd", "cc")
    p.wait("gpsimd", "d_kvi", nkvi)
    emit_band_gathers(0, wait_mm=False)

    ktband, cm_ktband = sbuf("ktband", [128, 64 * 128], BF16)
    probs, cm_probs = sbuf("probs", [128, 4 * 512], BF16)
    trineg, cm_trineg = sbuf("trineg_s", [128, 4 * 512], BF16)
    biastab, cm_biastab = sbuf("biastab_s", [128, HEADS * 64], F32)
    sacc, cm_sacc = sbuf("sacc", [128, 1024], BF16)
    mskbuf, cm_mskbuf = sbuf("mskbuf", [128, 512], F32)
    dbase, cm_dbase = sbuf("dbase_s", [128, 512], F32)
    epi, cm_epi = sbuf("epi", [128, 512], F32)
    rsum, cm_rsum = sbuf("rsum", [1, 512], F32)

    p.op("sync", lambda e: e.dma_start(
        out=trineg[:, :].rearrange("p (m f) -> p m f", m=4),
        in_=trineg_in.rearrange("m p f -> p m f")), inc="d_in", incv=16)
    p.op("sync", lambda e: e.dma_start(out=dbase[:, :], in_=dbase_in),
         inc="d_in", incv=16)
    nmisc = p.cnt["d_in"]

    if DBG:
        p.wait("sync", "cc")
        p.op("sync", lambda e: e.dma_start(out=dbg_kv, in_=kv_cc_out.ap()),
             inc="d_out", incv=16)
    PSC = (PS[0], PS[1], PS[2], PS[3])
    PAT = (PS[4], PS[5])
    PSU6 = PS[6]
    PBC = PS[7]

    for st in range(2):
        if st == 1:
            p.wait("sync", "sc")
        nb = p.op("sync", lambda e, st=st: e.dma_start(
            out=biastab[:, :], in_=bias_in[:, st * HEADS * 64:(st + 1) * HEADS * 64]),
            inc="d_bias", incv=16)
        jmin = 32 if st == 0 else 0
        if st == 1:
            emit_band_gathers(1, wait_mm=True, jlo=32)
        if st == 0:
            p.wait("tensor", "gpk0")
            p.wait("tensor", "gpk1")
        else:
            p.wait("tensor", "gpe")
            p.wait("tensor", "ve")
        ktev = [None] * 64
        for j in range(jmin, 64):
            bank = PS[6 + j % 2]
            if st == 1 and j == 32:
                p.wait("tensor", "gpk0")
                p.wait("tensor", "gpk1")
            if j >= jmin + 2:
                p.wait("tensor", "ve", ktev[j - 2])
            ntr = p.op("tensor", lambda e, j=j, b=bank: e.matmul(
                b[:, 0:128], kv_band[:, 256 * j:256 * j + 128], ident[:, :],
                start=True, stop=True), inc="mm")
            p.wait("vector", "mm", ntr)
            ktev[j] = p.op("vector", lambda e, j=j, b=bank: e.tensor_copy(
                ktband[:, 128 * j:128 * (j + 1)], b[:, 0:128]), inc="ve")
        p.wait("tensor", "ve", ktev[63])
        if st == 0:
            emit_band_gathers(1, wait_mm=False, jlo=0, jhi=32,
                              sems=("gpe", "gpe"))
        p.wait("scalar", "d_bias", nb)
        p.wait("vector", "d_in", nmisc)

        for h in range(HEADS):
            L = min(L_H[h], 64 - jmin)
            j0 = 64 - L
            seedp = h < SEED_MAX
            at = PAT[h % 2]
            su = PSU6[0:1, :]
            if h >= 2:
                p.wait("tensor", "ve")
            exp_scale = SCALE
            ready = {}
            sacc_n = {}
            last_sacc = [0]

            def slot(j):
                return (j - j0) % 4

            def emit_scores(j):
                sl = slot(j)
                sc_b = PSC[sl]
                if j - 4 >= j0:
                    p.wait("tensor", ready[j - 4][0], ready[j - 4][1])
                nmm = p.op("tensor", lambda e, b=sc_b, j=j, h=h, st=st: e.matmul(
                    b[:, :], ktband[:, 128 * j:128 * (j + 1)],
                    bigB[:, h * 1024 + st * 512: h * 1024 + (st + 1) * 512],
                    start=True, stop=True), inc="mm")
                diag = j >= 60
                src_b = sc_b
                if seedp or diag:
                    p.wait("vector", "mm", nmm)
                    p.wait("vector", "sc")
                    if seedp:
                        nv = p.op("vector", lambda e, b=sc_b, h=h: e.scalar_tensor_tensor(
                            mskbuf[:, :], dbase[:, :], float(SLOPES[h]) / SCALE,
                            b[:, :], ALU.mult, ALU.add), inc="ve")
                        if diag:
                            p.wait("vector", "ve", nv)
                            nv = p.op("vector", lambda e, j=j: e.tensor_add(
                                mskbuf[:, :], mskbuf[:, :],
                                trineg[:, 512 * (j - 60):512 * (j - 59)]), inc="ve")
                    else:
                        nv = p.op("vector", lambda e, b=sc_b, j=j: e.tensor_add(
                            mskbuf[:, :], b[:, :],
                            trineg[:, 512 * (j - 60):512 * (j - 59)]), inc="ve")
                    p.wait("scalar", "ve", nv)
                    src_b = mskbuf
                else:
                    p.wait("scalar", "mm", nmm)
                if j - 4 >= j0 and ((j - 4 - j0) // 2) in sacc_n:
                    p.wait("scalar", "ve", sacc_n[(j - 4 - j0) // 2])
                nsc = p.op("scalar", lambda e, b=src_b, sl=sl, j=j, h=h, sc=exp_scale:
                           e.activation(probs[:, 512 * sl:512 * (sl + 1)], b[:, :],
                                        AF.Exp, bias=biastab[:, h * 64 + j:h * 64 + j + 1],
                                        scale=sc), inc="sc")
                ready[j] = ("sc", nsc)

            def emit_pv(j):
                sl = slot(j)
                p.wait("tensor", ready[j][0], ready[j][1])
                p.op("tensor", lambda e, a=at, j=j, sl=sl, first=(j == j0): e.matmul(
                    a[:, :], kv_band[:, 256 * j + 128:256 * (j + 1)],
                    probs[:, 512 * sl:512 * (sl + 1)],
                    start=first, stop=(j == 63)),
                    inc="mm" if j == 63 else None)

            def emit_sacc(m, ja, jb):
                p.wait("vector", ready[ja][0], ready[ja][1])
                if jb is not None:
                    p.wait("vector", ready[jb][0], ready[jb][1])
                sl = slot(ja)
                width = 1024 if jb is not None else 512
                if m == 0:
                    if width == 1024:
                        n = p.op("vector", lambda e, sl=sl: e.tensor_copy(
                            sacc[:, 0:1024], probs[:, 512 * sl:512 * sl + 1024]),
                            inc="ve")
                    else:
                        n = p.op("vector", lambda e, sl=sl: e.tensor_copy(
                            sacc[:, 0:512], probs[:, 512 * sl:512 * (sl + 1)]),
                            inc="ve")
                        nz = p.op("vector", lambda e: e.memset(sacc[:, 512:1024], 0.0),
                                  inc="ve")
                        n = nz
                else:
                    p.wait("vector", "ve", last_sacc[0])
                    if width == 1024:
                        n = p.op("vector", lambda e, sl=sl: e.tensor_add(
                            sacc[:, 0:1024], sacc[:, 0:1024],
                            probs[:, 512 * sl:512 * sl + 1024]), inc="ve")
                    else:
                        n = p.op("vector", lambda e, sl=sl: e.tensor_add(
                            sacc[:, 0:512], sacc[:, 0:512],
                            probs[:, 512 * sl:512 * (sl + 1)]), inc="ve")
                sacc_n[m] = n
                last_sacc[0] = n

            emit_scores(j0)
            if j0 + 1 <= 63:
                emit_scores(j0 + 1)
            for j in range(j0, 64):
                if j + 2 <= 63:
                    emit_scores(j + 2)
                emit_pv(j)
                off = j - j0
                if off % 2 == 1:
                    emit_sacc(off // 2, j - 1, j)
                elif j == 63:
                    emit_sacc(off // 2, j, None)
            p.wait("tensor", "ve", last_sacc[0])
            p.op("tensor", lambda e, s=su: e.matmul(
                s, onescb[:, :], sacc[:, 0:512], start=True, stop=False))
            nsum = p.op("tensor", lambda e, s=su: e.matmul(
                s, onescb[:, :], sacc[:, 512:1024], start=False, stop=True), inc="mm")
            p.wait("vector", "mm", nsum)
            nv = vch(lambda e, s=su: e.reciprocal(rsum[0:1, :], s))
            p.wait("tensor", "ve", nv)
            nmm2 = p.op("tensor", lambda e: e.matmul(
                PBC[:, :], onesr32[:, :], rsum[0:1, :], start=True, stop=True), inc="mm")
            p.wait("vector", "mm", nmm2)
            vch(lambda e: e.tensor_copy(epi[:, :], PBC[:, :]))
            nv = vch(lambda e, a=at, h=h, st=st: e.tensor_mul(
                bigA[:, h * 1024 + st * 512: h * 1024 + (st + 1) * 512],
                a[:, :], epi[:, :]))

    p.barrier()
    if DBG:
        p.op("sync", lambda e: e.dma_start(out=dbg_att, in_=bigA[:, :]),
             inc="d_out", incv=16)
    for cm in (cm_rsum, cm_epi, cm_dbase, cm_mskbuf, cm_sacc, cm_biastab,
               cm_trineg, cm_probs, cm_ktband):
        cm.__exit__(None, None, None)
    cm_kvidx.__exit__(None, None, None)
    cm_kvband.__exit__(None, None, None)

    sstage, cm_sstage = sbuf("sstage", [128, 512], BF16)

    p.op("gpsimd", lambda e: e.dma_start(
        out=bigB[:, :].rearrange("p (g c f) -> p g c f", g=2, c=32)[:, 0:1],
        in_=wo_in[0:1].rearrange("g c p f -> p g c f")),
        inc="d_wg0", incv=16)
    for oc in range(8):
        half = oc % 2
        wo_sl = (lambda h_, half=half: bigB[:, half * 16384 + h_ * 512:
                                            half * 16384 + (h_ + 1) * 512])
        p.wait("tensor", "d_wg0" if half == 0 else "d_wg1")
        if oc + 1 < 8:
            if oc >= 1:
                p.wait("gpsimd", "mm")
            p.op("gpsimd", lambda e, oc=oc, hf=1 - half: e.dma_start(
                out=bigB[:, :].rearrange("p (g c f) -> p g c f", g=2, c=32)[:, hf:hf + 1],
                in_=wo_in[oc + 1:oc + 2].rearrange("g c p f -> p g c f")),
                inc="d_wg0" if (1 - half) == 0 else "d_wg1", incv=16)
        for qs in range(8):
            bank = PS[qs]
            p.wait("tensor", "sc")
            nmm = None
            for h in range(32):
                nmm = p.op("tensor", lambda e, b=bank, h=h, qs=qs, wo_sl=wo_sl: e.matmul(
                    b[:, :], bigA[:, h * 1024 + qs * 128: h * 1024 + (qs + 1) * 128],
                    wo_sl(h), start=(h == 0), stop=(h == 31)),
                    inc="mm" if h == 31 else None)
            p.wait("scalar", "mm", nmm)
            ns = p.op("scalar", lambda e, b=bank: e.activation(
                sstage[:, :], b[:, :], AF.Copy), inc="sc")
            p.wait("sync", "sc", ns)
            p.op("sync", lambda e, oc=oc, qs=qs: e.dma_start(
                out=park_s[oc, 128 * qs:128 * (qs + 1), :], in_=sstage[:, :]),
                inc="d_park", incv=16)
            p.wait("scalar", "d_park")

    p.barrier()
    cm_sstage.__exit__(None, None, None)

    wff, cm_wff = sbuf("wff", [128, D], F32)
    ta, cm_ta = sbuf("ta", [128, D], BF16)
    tb, cm_tb = sbuf("tb", [128, D], BF16)
    wtpp, cm_wtpp = sbuf("wtpp", [128, 32 * 128], BF16)
    gstage = wff[:, :].bitcast(BF16)[:, 0:D]
    fo = bigA[:, :].bitcast(F32)[:, 0:D]
    fx = bigA[:, :].bitcast(F32)[:, D:2 * D]
    fs = bigA[:, :].bitcast(BF16)[:, 4 * D:6 * D]
    p.wait("sync", "d_park")
    p.op("sync", lambda e: e.dma_start(out=bigA[:, :], in_=park_xh), inc="d_park", incv=16)
    nxh = p.cnt["d_park"]

    nmm = p.op("tensor", lambda e: e.matmul(PS[6][0:1, 0:1], smalls[:, 16:17],
                                            onesc32[:, :], start=True, stop=True),
               inc="mm")
    p.wait("vector", "mm", nmm)
    vch(lambda e: e.tensor_copy(smalls[0:1, 18:19], PS[6][0:1, 0:1]))
    p.wait("tensor", "ve")
    nmm = p.op("tensor", lambda e: e.matmul(PS[6][:, 0:1], onesr32[:, :],
                                            smalls[0:1, 18:19], start=True, stop=True),
               inc="mm")
    p.wait("vector", "mm", nmm)
    vch(lambda e: e.tensor_scalar(smalls[:, 19:20], PS[6][:, 0:1],
                                  0.5 / (D * D), None, ALU.mult))
    vch(lambda e: e.tensor_scalar(smalls[:, 20:21], smalls[:, 19:20], -1.0, None,
                                  ALU.mult))
    vch(lambda e: e.tensor_scalar(smalls[:, 21:22], smalls[:, 19:20], 2.0, None,
                                  ALU.mult))

    HW = D // 2
    p.op("sync", lambda e: e.dma_start(out=wff[:, 0:HW], in_=wffn_in[0][:, 0:HW]),
         inc="d_ws0", incv=16)
    p.op("sync", lambda e: e.dma_start(out=wff[:, HW:D], in_=wffn_in[0][:, HW:D]),
         inc="d_ws1", incv=16)
    p.wait("tensor", "d_park", nxh)
    for oc in range(32):
        for hf in range(2):
            sl_ = slice(hf * HW, (hf + 1) * HW)
            p.wait("vector", "d_ws0" if hf == 0 else "d_ws1")
            vch(lambda e, sl_=sl_: e.tensor_scalar(
                tb[:, sl_], wff[:, sl_], smalls[:, 20:21], None, ALU.is_lt))
            nv = vch(lambda e, sl_=sl_: e.scalar_tensor_tensor(
                ta[:, sl_], wff[:, sl_], smalls[:, 19:20], tb[:, sl_],
                ALU.is_gt, ALU.subtract))
            if oc + 1 < 32:
                p.wait("sync", "ve", nv)
                p.op("sync", lambda e, oc=oc, sl_=sl_: e.dma_start(
                    out=wff[:, sl_], in_=wffn_in[oc + 1][:, sl_]),
                    inc="d_ws0" if hf == 0 else "d_ws1", incv=16)
        p.wait("tensor", "ve", nv)
        for grp in range(8):
            bank = PS[grp % 2]
            for q4 in range(4):
                ci = grp * 4 + q4
                p.op("tensor", lambda e, b=bank, q4=q4, ci=ci: e.matmul(
                    b[:, 128 * q4:128 * (q4 + 1)], ta[:, 128 * ci:128 * (ci + 1)],
                    ident[:, :], start=True, stop=True),
                    inc="mm" if q4 == 3 else None)
            p.wait("vector", "mm")
            nv2 = p.op("vector", lambda e, b=bank, grp=grp: e.tensor_copy(
                wtpp[:, 512 * grp:512 * (grp + 1)], b[:, :]), inc="ve")
            p.wait("tensor", "ve", nv2)
        for qh in range(2):
            bank = PS[2 + qh]
            p.wait("tensor", "sc")
            nmm = None
            for d in range(32):
                nmm = p.op("tensor", lambda e, b=bank, d=d, qh=qh: e.matmul(
                    b[:, :], wtpp[:, 128 * d:128 * (d + 1)],
                    bigA[:, d * 1024 + qh * 512: d * 1024 + (qh + 1) * 512],
                    start=(d == 0), stop=(d == 31)), inc="mm" if d == 31 else None)
            p.wait("scalar", "mm", nmm)
            p.op("scalar", lambda e, b=bank, oc=oc, qh=qh: e.activation(
                bigB[:, oc * 1024 + qh * 512: oc * 1024 + (qh + 1) * 512],
                b[:, :], AF.Copy), inc="sc")

    p.wait("tensor", "sc")
    fxn = [0] * NTILE
    fsn = [0] * NTILE
    stn = [0] * NTILE
    p.wait("sync", "d_park")
    p.wait("sync", "sc")
    fxn[0] = p.op("sync", lambda e: e.dma_start(out=fx, in_=x_in[0:128, :]),
                  inc="d_fx", incv=16)
    for t0 in range(2):
        fsn[t0] = p.op("sync", lambda e, t0=t0, b=t0 % 2: e.dma_start(
            out=fs[:, b * D:(b + 1) * D].rearrange("p (o f) -> p o f", o=8),
            in_=park_s[:, 128 * t0:128 * (t0 + 1), :].transpose([1, 0, 2])),
            inc="d_f0" if t0 % 2 == 0 else "d_f1", incv=16)
    for t in range(NTILE):
        if t >= 1:
            p.wait("scalar", "d_out", stn[t - 1])
        for grp in range(8):
            bank = PS[4 + grp % 2]
            for q4 in range(4):
                ci = grp * 4 + q4
                p.op("tensor", lambda e, b=bank, q4=q4, ci=ci, t=t: e.matmul(
                    b[:, 128 * q4:128 * (q4 + 1)], hbd[:, :],
                    bigB[:, ci * 1024 + 128 * t: ci * 1024 + 128 * (t + 1)],
                    start=True, stop=True), inc="mm" if q4 == 3 else None)
            eng, sem = ("vector", "ve") if grp % 2 == 0 else ("scalar", "sc")
            p.wait(eng, "mm")
            if eng == "vector":
                nv = p.op(eng, lambda e, b=bank, grp=grp: e.tensor_copy(
                    ta[:, 512 * grp:512 * (grp + 1)], b[:, :]), inc=sem)
            else:
                nv = p.op(eng, lambda e, b=bank, grp=grp: e.activation(
                    ta[:, 512 * grp:512 * (grp + 1)], b[:, :], AF.Copy), inc=sem)
            p.wait("tensor", sem, nv)
        tb_v = tb.ap().rearrange("p (j m i2) -> p m j i2", j=64, m=32, i2=2)
        for grp in range(8):
            bank = PS[6 + grp % 2]
            for q4 in range(4):
                ci = grp * 4 + q4
                p.op("tensor", lambda e, b=bank, q4=q4, ci=ci: e.matmul(
                    b[:, 128 * q4:128 * (q4 + 1)], ta[:, 128 * ci:128 * (ci + 1)],
                    ident[:, :], start=True, stop=True), inc="mm" if q4 == 3 else None)
            p.wait("vector", "mm")
            src = PS[6 + grp % 2].ap().rearrange(
                "p (m i2 j) -> p m j i2", m=4, i2=2, j=64)
            dst = tb_v[:, 4 * grp:4 * (grp + 1)]
            nv = p.op("vector", lambda e, s=src, d=dst: e.tensor_copy(d, s), inc="ve")
            p.wait("tensor", "ve", nv)
        for grp in range(8):
            bank = PS[0 + grp % 2]
            for q4 in range(4):
                ci = grp * 4 + q4
                p.op("tensor", lambda e, b=bank, q4=q4, ci=ci: e.matmul(
                    b[:, 128 * q4:128 * (q4 + 1)], tb[:, 128 * ci:128 * (ci + 1)],
                    ident[:, :], start=True, stop=True), inc="mm" if q4 == 3 else None)
            eng, sem = ("vector", "ve") if grp % 2 == 0 else ("scalar", "sc")
            p.wait(eng, "mm")
            if eng == "vector":
                nv = p.op(eng, lambda e, b=bank, grp=grp: e.tensor_copy(
                    ta[:, 512 * grp:512 * (grp + 1)], b[:, :]), inc=sem)
            else:
                nv = p.op(eng, lambda e, b=bank, grp=grp: e.activation(
                    ta[:, 512 * grp:512 * (grp + 1)], b[:, :], AF.Copy), inc=sem)
            p.wait("tensor", sem, nv)
        gs_v = gstage.rearrange("p (i g m o2) -> p g m i o2", i=64, g=8, m=4, o2=2)
        for grp in range(8):
            bank = PS[2 + grp % 2]
            for q4 in range(4):
                ci = grp * 4 + q4
                p.op("tensor", lambda e, b=bank, q4=q4, ci=ci: e.matmul(
                    b[:, 128 * q4:128 * (q4 + 1)], ta[:, 128 * ci:128 * (ci + 1)],
                    hbd[:, :], start=True, stop=True), inc="mm" if q4 == 3 else None)
            p.wait("scalar", "mm")
            src = PS[2 + grp % 2].ap().rearrange(
                "p (g m o2 i) -> p g m i o2", g=1, m=4, o2=2, i=64)
            dst = gs_v[:, grp:grp + 1]
            ns = p.op("scalar", lambda e, s=src, d=dst: e.activation(
                d, s, AF.Silu, scale=smalls[:, 21:22]), inc="sc")
            p.wait("tensor", "sc", ns)
        p.wait("vector", "d_f0" if t % 2 == 0 else "d_f1", fsn[t])
        p.wait("vector", "d_fx", fxn[t])
        p.wait("vector", "sc")
        if t >= 1:
            p.wait("vector", "d_out", stn[t - 1])
        vch(lambda e, b=t % 2: e.tensor_mul(fo, fs[:, b * D:(b + 1) * D], gstage))
        nv = vch(lambda e: e.tensor_add(fo, fo, fx))
        p.wait("sync", "ve", nv)
        stn[t] = p.op("sync", lambda e, t=t: e.dma_start(
            out=out_d[128 * t:128 * (t + 1), :], in_=fo), inc="d_out", incv=16)
        if t + 1 < NTILE:
            fxn[t + 1] = p.op("sync", lambda e, t=t: e.dma_start(
                out=fx, in_=x_in[128 * (t + 1):128 * (t + 2), :]),
                inc="d_fx", incv=16)
        if t + 2 < NTILE:
            fsn[t + 2] = p.op("sync", lambda e, t=t, b=t % 2: e.dma_start(
                out=fs[:, b * D:(b + 1) * D].rearrange("p (o f) -> p o f", o=8),
                in_=park_s[:, 128 * (t + 2):128 * (t + 3), :].transpose([1, 0, 2])),
                inc="d_f0" if t % 2 == 0 else "d_f1", incv=16)

    for cm in (cm_wtpp, cm_tb, cm_ta, cm_wff):
        cm.__exit__(None, None, None)

    p.wait("sync", "d_out")

    blk_cm = nc.Block()
    blk = blk_cm.__enter__()
    for name in ENGS:
        thunks = p.stream[name]
        getattr(blk, name)(lambda e, th=thunks: [f(e) for f in th])
    blk_cm.__exit__(None, None, None)

    for cm in reversed(ctx):
        cm.__exit__(None, None, None)
    return nc



def _h64():
    m = np.array([[1.0]])
    for _ in range(6):
        m = np.block([[m, m], [m, -m]])
    return m


def _host_constants():
    H64 = _h64()
    hbd = np.zeros((128, 128), np.float32)
    hbd[:64, :64] = H64
    hbd[64:, 64:] = H64
    ident = np.eye(128, dtype=np.float32)
    pk = np.arange(128)[:, None]
    fq = np.arange(512)[None, :]
    trineg = np.stack([np.where(j2 * 128 + pk <= fq, 0.0, NEG).astype(np.float32)
                       for j2 in range(4)])
    pkf = np.arange(128, dtype=np.float64)[:, None] - np.arange(512, dtype=np.float64)[None, :]
    return dict(
        dbase=pkf.astype(np.float32),
        hbd=hbd.astype(BF), ident=ident.astype(BF),
        trineg=trineg.astype(BF),
        ones_col_bf=np.ones((128, 1), BF), ones_col_f32=np.ones((128, 1), np.float32),
        ones_row_f32=np.ones((1, 128), np.float32),
    )


def _core_tables(c):
    stripes = (c, 15 - c)
    bias = np.zeros((128, 2, HEADS, 64), np.float64)
    kvidx = np.zeros((128, 2, 64), np.int64)
    pk = np.arange(128)
    for st, sidx in enumerate(stripes):
        E = 4 * sidx + 3
        q0 = 512 * sidx
        for j in range(64):
            ktg = E - 63 + j
            if ktg < 0:
                kvidx[:, st, j] = 0
                bias[:, st, :, j] = NEG
                continue
            g = 128 * ktg + pk
            srow = g // 512
            r = np.where(srow < 8, srow, 15 - srow)
            local = np.where(srow < 8, g - 512 * r, 512 + (g - 512 * (15 - r)))
            kvidx[:, st, j] = 1024 * r + local
            for h in range(HEADS):
                s = float(SLOPES[h])
                if h < SEED_MAX:
                    bias[:, st, h, j] = s * (128.0 * ktg - q0)
                else:
                    bias[:, st, h, j] = s * (128.0 * ktg + pk - (q0 + 511.0))
    return (bias.reshape(128, 2 * HEADS * 64).astype(np.float32),
            kvidx.reshape(128, 2 * 64).astype(np.int32))


DPERM = np.array([(dpp % 64) * 64 + dpp // 64 for dpp in range(D)])

_NC_CACHE = {}


def kernel(L0_attn_hidden_states, L0_attn_Wqkv, L0_attn_Wo, L0_ffn_gamma,
           L0_ffn_beta, L0_ffn_weight):
    x = np.ascontiguousarray(np.asarray(L0_attn_hidden_states, dtype=np.float32)
                             .reshape(S, D))
    wqkv = np.asarray(L0_attn_Wqkv, dtype=np.float32)
    wo = np.asarray(L0_attn_Wo, dtype=np.float32)
    gamma = np.asarray(L0_ffn_gamma, dtype=np.float32)
    beta = np.asarray(L0_ffn_beta, dtype=np.float32)
    wffn = np.asarray(L0_ffn_weight, dtype=np.float32)

    wqkv_t = np.ascontiguousarray(
        wqkv.reshape(32, 128, 34, 128).transpose(2, 0, 1, 3))
    wo_t = np.ascontiguousarray(
        wo.reshape(32, 128, 8, 512).transpose(2, 0, 1, 3))
    wffn_t = np.ascontiguousarray(
        wffn[:, DPERM].reshape(32, 128, D))
    gamma_b = np.ascontiguousarray(np.broadcast_to(gamma[None, :], (128, D)))
    beta_b = np.ascontiguousarray(np.broadcast_to(beta[None, :], (128, D)))

    consts = _host_constants()

    in_maps = []
    for c in range(NCORE):
        rows = np.r_[512 * c:512 * c + 512, 512 * (15 - c):512 * (15 - c) + 512]
        bias_tab, kv_idx = _core_tables(c)
        m = dict(
            x_sh=np.ascontiguousarray(x[rows]),
            wqkv_t=wqkv_t, wo_t=wo_t, wffn_t=wffn_t,
            gamma_b=gamma_b, beta_b=beta_b,
            bias_tab=bias_tab, kv_idx=kv_idx,
            dbase=consts["dbase"], hbd=consts["hbd"], ident=consts["ident"],
            trineg=consts["trineg"],
            ones_col_bf=consts["ones_col_bf"], ones_col_f32=consts["ones_col_f32"],
            ones_row_f32=consts["ones_row_f32"],
        )
        in_maps.append(m)

    if "nc" not in _NC_CACHE:
        _NC_CACHE["nc"] = build_program()
    nc = _NC_CACHE["nc"]

    res = run_bass_kernel_spmd(nc, in_maps, core_ids=list(range(NCORE)))

    out = np.empty((S, D), np.float32)
    for c in range(NCORE):
        o = res.results[c]["out_sh"]
        out[512 * c:512 * c + 512] = o[0:512]
        out[512 * (15 - c):512 * (15 - c) + 512] = o[512:1024]
    return out.reshape(1, S, D)


if __name__ == "__main__":
    nc = build_program()
    print("build ok")
